# version 18
# baseline (speedup 1.0000x reference)
import contextlib
import math

import numpy as np
import ml_dtypes

import concourse.bass as bass
import concourse.mybir as mybir
from concourse.bass_utils import run_bass_kernel_spmd

F32 = mybir.dt.float32
BF16 = mybir.dt.bfloat16
AF = mybir.ActivationFunctionType
ALU = mybir.AluOpType
AX = mybir.AxisListType

T = 2048
C = 1024
NCORE = 8
BF = ml_dtypes.bfloat16

ENGINES = ("tensor", "vector", "scalar", "gpsimd", "sync")


class Prog:

    def __init__(self, name, nc=None):
        self.nc = nc or bass.Bass("TRN2", target_bir_lowering=False, debug=False)
        self.es = contextlib.ExitStack()
        self.queues = {e: [] for e in ENGINES}
        self.sems = {}
        self.cnt = {e: 0 for e in ENGINES}
        self.writer = {}
        self.readers = {}
        self.name = name
        self._consts = {}
        self.hiwater = {}
        self.dma_chans = set()

    def _sem(self, e):
        if e not in self.sems:
            self.sems[e] = self.es.enter_context(
                self.nc.semaphore(self.name + "_s_" + e))
            self.cnt.setdefault(e, 0)
        return self.sems[e]

    def dram(self, name, shape, dtype, kind):
        return self.nc.dram_tensor(name, list(shape), dtype, kind=kind).ap()

    def sb(self, name, shape, dtype):
        return self.es.enter_context(
            self.nc.sbuf_tensor(self.name + "_" + name, list(shape), dtype))

    def ps(self, name, shape, dtype=F32):
        return self.es.enter_context(
            self.nc.psum_tensor(self.name + "_" + name, list(shape), dtype))

    def constv(self, val):
        if val not in self._consts:
            name = "cv%d" % len(self._consts)
            t = self.sb(name, [128, 1], F32)
            self.memset(t[:, :], float(val), writes=[name])
            self._consts[val] = (t, name)
        return self._consts[val]

    def op(self, eng, fn, reads=(), writes=(), dma=False, chan=None):
        ch = chan or eng
        waits = {}

        def _add(e2, c2):
            if e2 == ch == "tensor":
                return
            waits[e2] = max(waits.get(e2, 0), c2)

        for k in reads:
            w = self.writer.get(k)
            if w:
                _add(*w)
            if k.startswith("ps"):
                for e2, c2 in self.readers.get(k, {}).items():
                    _add(e2, c2)
        for k in writes:
            w = self.writer.get(k)
            if w:
                _add(*w)
            for e2, c2 in self.readers.get(k, {}).items():
                _add(e2, c2)
        if dma:
            self.dma_chans.add(ch)
            if chan is None and self.cnt.get(ch, 0):
                waits[ch] = max(waits.get(ch, 0), self.cnt[ch])
        for e2 in list(waits):
            if e2 in self.dma_chans:
                waits[e2] = max(waits[e2], self.cnt.get(e2, 0))
        inc = 16 if dma else 1
        self._sem(ch)
        self.cnt[ch] += inc
        cnt_after = self.cnt[ch]
        mysem = self.sems[ch]
        for e2 in waits:
            self._sem(e2)

        hw = self.hiwater.setdefault(eng, {})
        pruned = {}
        for e2, v in waits.items():
            if e2 == ch and v >= cnt_after:
                v = cnt_after - inc
            if hw.get(e2, -1) >= v:
                continue
            pruned[e2] = v
            hw[e2] = v

        def emit(engobj, fn=fn, waits=pruned, inc=inc):
            for e2, v in waits.items():
                engobj.wait_ge(self.sems[e2], v)
            ins = fn(engobj)
            ins.then_inc(mysem, inc)

        self.queues[eng].append(emit)
        for k in reads:
            self.readers.setdefault(k, {})[ch] = cnt_after
        for k in writes:
            self.writer[k] = (ch, cnt_after)
            self.readers[k] = {}
        return cnt_after

    def dma(self, q, out, in_, reads=(), writes=(), allow_slow=False,
            chan=None):
        self.op(q, lambda g: g.dma_start(
            out, in_, allow_slow_non_contiguous=allow_slow),
            reads=reads, writes=writes, dma=True, chan=chan)

    def dma_add(self, q, out, in_, reads=(), writes=()):
        self.op(q, lambda g: g.dma_start(out, in_, accum_op=ALU.add),
                reads=reads, writes=writes, dma=True)

    def mm(self, out, lhsT, rhs, start, stop, reads=(), writes=(), tp=None):
        self.op("tensor",
                lambda t: t.matmul(out, lhsT, rhs, start=start, stop=stop,
                                   tile_position=tp),
                reads=reads, writes=writes)

    def trans(self, out, in_, ident, reads=(), writes=()):
        self.op("tensor", lambda t: t.transpose(out, in_, ident),
                reads=reads, writes=writes)

    def act(self, out, in_, func, bias=0.0, scale=1.0, accum_out=None,
            reads=(), writes=()):
        if isinstance(bias, float) and bias not in (0.0, 1.0):
            t, nm = self.constv(bias)
            nparts = int(out.shape[0])
            bias = t[0:nparts, 0:1]
            reads = list(reads) + [nm]
        self.op("scalar",
                lambda s: s.activation(out, in_, func, bias=bias, scale=scale,
                                       accum_out=accum_out),
                reads=reads, writes=writes)

    def copy(self, eng, out, in_, reads=(), writes=()):
        if eng == "scalar":
            self.op(eng, lambda s: s.copy(out, in_), reads=reads, writes=writes)
        else:
            self.op(eng, lambda v: v.tensor_copy(out, in_), reads=reads,
                    writes=writes)

    def tt(self, out, a, b, alu, reads=(), writes=()):
        self.op("vector", lambda v: v.tensor_tensor(out, a, b, op=alu),
                reads=reads, writes=writes)

    def ts(self, out, a, s1, s2, op0, op1=ALU.bypass, reads=(), writes=()):
        self.op("vector",
                lambda v: v.tensor_scalar(out, a, s1, s2, op0=op0, op1=op1),
                reads=reads, writes=writes)

    def stt(self, out, in0, scalar, in1, op0, op1, reads=(), writes=()):
        self.op("vector",
                lambda v: v.scalar_tensor_tensor(out, in0, scalar, in1,
                                                 op0=op0, op1=op1),
                reads=reads, writes=writes)

    def red(self, out, in_, alu, negate=False, reads=(), writes=()):
        self.op("vector",
                lambda v: v.tensor_reduce(out, in_, axis=AX.X, op=alu,
                                          negate=negate),
                reads=reads, writes=writes)

    def recip(self, out, in_, reads=(), writes=()):
        self.op("vector", lambda v: v.reciprocal(out, in_), reads=reads,
                writes=writes)

    def memset(self, ap, val, reads=(), writes=()):
        self.op("vector", lambda v: v.memset(ap, val), reads=reads,
                writes=writes)

    def finish(self):
        for e in list(self.cnt):
            if self.cnt[e] == 0:
                continue
            self._sem(e)
        fin = {e: c for e, c in self.cnt.items() if c > 0}

        def final(g, fin=fin):
            for e, v in fin.items():
                g.wait_ge(self.sems[e], v)

        with self.nc.Block() as block:
            for e in ENGINES:
                q = self.queues[e]
                if not q and e != "gpsimd":
                    continue
                body = getattr(block, e)

                def run(engobj, q=q, last_e=(e == "gpsimd")):
                    for emit in q:
                        emit(engobj)
                    if last_e:
                        final(engobj)
                body(run)
        for e, c in self.cnt.items():
            assert c < 60000, f"sem overflow risk on {e}: {c}"
        assert len(self.sems) <= 24, f"too many sems: {len(self.sems)}"
        self.es.close()
        return self.nc


def _emit_fm_out(p, tm_tile, tm_key, eye, dst_fm, tile_i, ps_trs, stgs):
    stg, stg_key = stgs[tile_i % 2]
    for ct in range(8):
        ps_tr, psk = ps_trs[ct % 2]
        p.trans(ps_tr[:, 0:128], tm_tile[:, ct * 128:(ct + 1) * 128],
                eye[:, :], reads=[tm_key, "eye"], writes=[psk])
        p.copy("vector", stg[:, ct * 128:(ct + 1) * 128], ps_tr[:, 0:128],
               reads=[psk], writes=[stg_key])
    dst = bass.AP(dst_fm.tensor, tile_i * 128, [[T, 128], [128 * T, 8], [1, 128]])
    srcv = bass.AP(stg, 0, [[1024, 128], [128, 8], [1, 128]])
    p.dma("sync", dst, srcv, reads=[stg_key], chan="sync#t" + str(tile_i % 2))



def emit_norm(p, kind, x, gv, bv, eyef, xnF_o, xinF_o):
    bufs = []
    for par in range(2):
        s = str(par)
        bufs.append(dict(
            xt=(p.sb("xt" + s, [128, C], F32), "xt" + s),
            d=(p.sb("d" + s, [128, C], F32), "d" + s),
            sq=(p.sb("sq" + s, [128, C], F32), "sq" + s),
            xn_t=(p.sb("xn_t" + s, [128, C], F32), "xn_t" + s),
            xin_t=(p.sb("xin_t" + s, [128, C], F32), "xin_t" + s),
            u=(p.sb("u" + s, [128, C], F32), "u" + s),
            m=(p.sb("m" + s, [128, 1], F32), "m" + s),
            var=(p.sb("var" + s, [128, 1], F32), "var" + s),
            st=(p.sb("st" + s, [128, 1], F32), "st" + s),
            rstd=(p.sb("rstd" + s, [128, 1], F32), "rstd" + s)))
    grow = p.sb("grow", [1, C], F32)
    brow = p.sb("brow", [1, C], F32)
    ones1 = p.sb("ones1", [1, 128], F32)
    gb = p.sb("gb", [128, C], F32)
    bb = p.sb("bb", [128, C], F32)
    eye = p.sb("eye", [128, 128], F32)
    stg0 = p.sb("stg0", [128, 1024], F32)
    stg1 = p.sb("stg1", [128, 1024], F32)
    psb = p.ps("psb", [128, 512], F32)
    ps_tr = p.ps("ps_tr", [128, 512], F32)
    ps_tr2 = p.ps("ps_tr2", [128, 512], F32)
    ps_trs = [(ps_tr, "ps_tr"), (ps_tr2, "ps_tr2")]
    stgs = [(stg0, "stg0"), (stg1, "stg1")]

    p.dma("gpsimd", grow[:, :], gv, writes=["grow"])
    p.dma("gpsimd", brow[:, :], bv, writes=["brow"])
    p.dma("gpsimd", eye[:, :], eyef, writes=["eye"])
    p.memset(ones1[:, :], 1.0, writes=["ones1"])
    for j in range(C // 512):
        sl = slice(j * 512, (j + 1) * 512)
        p.mm(psb[:, :], ones1[:, :], grow[:, sl], True, True,
             reads=["ones1", "grow"], writes=["psb"])
        p.copy("vector", gb[:, sl], psb[:, :], reads=["psb"], writes=["gb"])
        p.mm(psb[:, :], ones1[:, :], brow[:, sl], True, True,
             reads=["ones1", "brow"], writes=["psb"])
        p.copy("vector", bb[:, sl], psb[:, :], reads=["psb"], writes=["bb"])

    inv_c = 1.0 / C
    for i in range(T // 128):
        B = bufs[i % 2]
        xt, xtk = B["xt"]; d, dk = B["d"]; sq, sqk = B["sq"]
        xn_t, xnk = B["xn_t"]; xin_t, xik = B["xin_t"]; u, uk = B["u"]
        m, mk = B["m"]; var, vk = B["var"]; st, stk = B["st"]
        rstd, rk = B["rstd"]
        rows = slice(i * 128, (i + 1) * 128)
        p.dma("gpsimd", xt[:, :], x[rows, :], writes=[xtk],
              chan="gpsimd#n" + str(i % 2))
        p.red(m[:, :], xt[:, :], ALU.add, reads=[xtk], writes=[mk])
        p.ts(m[:, :], m[:, :], inv_c, None, ALU.mult, reads=[mk], writes=[mk])
        p.ts(d[:, :], xt[:, :], m[:, 0:1], None, ALU.subtract,
             reads=[xtk, mk], writes=[dk])
        p.act(sq[:, :], d[:, :], AF.Square, reads=[dk], writes=[sqk])
        p.red(var[:, :], sq[:, :], ALU.add, reads=[sqk], writes=[vk])
        p.ts(var[:, :], var[:, :], inv_c, None, ALU.mult, reads=[vk],
             writes=[vk])
        p.act(st[:, :], var[:, :], AF.Sqrt, bias=1e-5, reads=[vk],
              writes=[stk])
        p.recip(rstd[:, :], st[:, :], reads=[stk], writes=[rk])
        p.ts(xn_t[:, :], d[:, :], rstd[:, 0:1], None, ALU.mult,
             reads=[dk, rk], writes=[xnk])
        _emit_fm_out(p, xn_t, xnk, eye, xnF_o, i, ps_trs, stgs)
        if kind == "ln2":
            p.red(m[:, :], xn_t[:, :], ALU.add, reads=[xnk], writes=[mk])
            p.ts(m[:, :], m[:, :], inv_c, None, ALU.mult, reads=[mk],
                 writes=[mk])
            p.ts(d[:, :], xn_t[:, :], m[:, 0:1], None, ALU.subtract,
                 reads=[xnk, mk], writes=[dk])
            p.act(sq[:, :], d[:, :], AF.Square, reads=[dk], writes=[sqk])
            p.red(var[:, :], sq[:, :], ALU.add, reads=[sqk], writes=[vk])
            p.ts(var[:, :], var[:, :], inv_c, None, ALU.mult, reads=[vk],
                 writes=[vk])
            p.act(st[:, :], var[:, :], AF.Sqrt, bias=1e-5, reads=[vk],
                  writes=[stk])
            p.recip(rstd[:, :], st[:, :], reads=[stk], writes=[rk])
            p.ts(u[:, :], d[:, :], rstd[:, 0:1], None, ALU.mult,
                 reads=[dk, rk], writes=[uk])
            p.tt(u[:, :], u[:, :], gb[:, :], ALU.mult, reads=[uk, "gb"],
                 writes=[uk])
            p.tt(xin_t[:, :], u[:, :], bb[:, :], ALU.add, reads=[uk, "bb"],
                 writes=[xik])
        else:
            p.act(sq[:, :], xn_t[:, :], AF.Square, reads=[xnk],
                  writes=[sqk])
            p.red(var[:, :], sq[:, :], ALU.add, reads=[sqk], writes=[vk])
            p.ts(var[:, :], var[:, :], inv_c, None, ALU.mult, reads=[vk],
                 writes=[vk])
            p.act(st[:, :], var[:, :], AF.Sqrt, bias=1e-5, reads=[vk],
                  writes=[stk])
            p.recip(rstd[:, :], st[:, :], reads=[stk], writes=[rk])
            p.ts(u[:, :], xn_t[:, :], rstd[:, 0:1], None, ALU.mult,
                 reads=[xnk, rk], writes=[uk])
            p.tt(xin_t[:, :], u[:, :], gb[:, :], ALU.mult, reads=[uk, "gb"],
                 writes=[xik])
        _emit_fm_out(p, xin_t, xik, eye, xinF_o, i, ps_trs, stgs)


def emit_router(p, inF, W, b, eyef, mgT_o=None, probsT_o=None):
    xf = p.sb("xf", [128, 8 * T], F32)
    wsb = p.sb("wsb", [128, 64], F32)
    brow = p.sb("brow", [1, 8], F32)
    ones1 = p.sb("ones1", [1, 128], F32)
    pe = p.sb("pe", [128, 8], F32)
    sums = p.sb("sums", [128, 1], F32)
    rs = p.sb("rs", [128, 1], F32)
    pr = p.sb("pr", [128, 8], F32)
    gate = p.sb("gate", [128, 1], F32)
    msk = p.sb("msk", [128, 8], F32)
    mgt = p.sb("mgt", [128, 8], F32)
    nm = p.sb("nm", [128, 1], F32)
    eye = p.sb("eye", [128, 128], F32)
    mgT_sb = p.sb("mgT_sb", [8, T], F32)
    prT_sb = p.sb("prT_sb", [8, T], F32)
    pl = p.ps("ps_l", [128, 8], F32)
    ps_tr = p.ps("ps_tr", [128, 512], F32)

    for kc in range(8):
        p.dma("gpsimd", xf[:, kc * T:(kc + 1) * T],
              inF[kc * 128:(kc + 1) * 128, :], writes=["xf"])
        p.dma("gpsimd", wsb[:, kc * 8:(kc + 1) * 8],
              W[kc * 128:(kc + 1) * 128, :], writes=["wsb"])
    p.dma("gpsimd", brow[:, :], b, writes=["brow"])
    p.dma("gpsimd", eye[:, :], eyef, writes=["eye"])
    p.memset(ones1[:, :], 1.0, writes=["ones1"])

    for i in range(T // 128):
        for kc in range(8):
            p.mm(pl[:, :], xf[:, kc * T + i * 128: kc * T + (i + 1) * 128],
                 wsb[:, kc * 8:(kc + 1) * 8], kc == 0, False,
                 reads=["xf", "wsb"], writes=["ps_l"])
        p.mm(pl[:, :], ones1[:, :], brow[:, :], False, True,
             reads=["ones1", "brow"], writes=["ps_l"])
        p.red(nm[:, :], pl[:, :], ALU.max, negate=True, reads=["ps_l"],
              writes=["nm"])
        p.act(pe[:, :], pl[:, :], AF.Exp, bias=nm[:, 0:1],
              accum_out=sums[:, 0:1], reads=["ps_l", "nm"],
              writes=["pe", "sums"])
        p.recip(rs[:, :], sums[:, :], reads=["sums"], writes=["rs"])
        p.ts(pr[:, :], pe[:, :], rs[:, 0:1], None, ALU.mult,
             reads=["pe", "rs"], writes=["pr"])
        if probsT_o is not None:
            p.trans(ps_tr[0:8, 0:128], pr[:, :], eye[:, :],
                    reads=["pr", "eye"], writes=["ps_tr"])
            p.copy("vector", prT_sb[:, i * 128:(i + 1) * 128],
                   ps_tr[0:8, 0:128], reads=["ps_tr"], writes=["prT_sb"])
        if mgT_o is not None:
            p.red(gate[:, :], pr[:, :], ALU.max, reads=["pr"], writes=["gate"])
            p.ts(msk[:, :], pr[:, :], gate[:, 0:1], None, ALU.is_ge,
                 reads=["pr", "gate"], writes=["msk"])
            p.ts(mgt[:, :], msk[:, :], gate[:, 0:1], None, ALU.mult,
                 reads=["msk", "gate"], writes=["mgt"])
            p.trans(ps_tr[0:8, 0:128], mgt[:, :], eye[:, :],
                    reads=["mgt", "eye"], writes=["ps_tr"])
            p.copy("vector", mgT_sb[:, i * 128:(i + 1) * 128],
                   ps_tr[0:8, 0:128], reads=["ps_tr"], writes=["mgT_sb"])
    if mgT_o is not None:
        p.dma("sync", mgT_o, mgT_sb[:, :], reads=["mgT_sb"])
    if probsT_o is not None:
        p.dma("sync", probsT_o, prT_sb[:, :], reads=["prT_sb"])


def emit_attn0(p, xnF, wqT, wkT, wvT, woT, s_col, cosI, sinI, P128, onesH,
               onesHT, eyef, sF_o):
    xn_bf = p.sb("xn_bf", [128, 8 * T], BF16)
    w_bf = p.sb("w_bf", [128, 8 * C], BF16)
    q_bf = p.sb("q_bf", [128, 8 * T], BF16)
    k_bf = p.sb("k_bf", [128, 8 * T], BF16)
    v_bf = p.sb("v_bf", [128, 16 * C], BF16)
    cosb = p.sb("cosb", [128, T], F32)
    sinb = p.sb("sinb", [128, T], F32)
    exp_bf = p.sb("exp_bf", [128, 16 * 512], BF16)
    nmaxb = p.sb("nmaxb", [128, 512], F32)
    tsum0 = p.sb("tsum0", [128, 512], F32)
    tsum1 = p.sb("tsum1", [128, 512], F32)
    p128 = p.sb("p128", [128, 128], F32)
    oh = p.sb("oh", [128, 2], F32)
    oht = p.sb("oht", [2, 128], F32)
    eye = p.sb("eye", [128, 128], F32)
    scol = p.sb("scol", [128, 8], F32)
    ksc = p.sb("ksc", [128, 8], F32)
    ones1 = p.sb("ones1", [1, 128], F32)
    onesk = p.sb("onesk", [128, 1], BF16)
    qraw = p.sb("qraw", [128, 512], F32)
    t1 = p.sb("t1", [128, 512], F32)
    t2 = p.sb("t2", [128, 512], F32)
    qr = p.sb("qr", [128, 512], F32)
    sqv = p.sb("sqv", [128, 512], F32)
    stv = p.sb("stv", [2, 512], F32)
    rnv = p.sb("rnv", [2, 512], F32)
    maxJP = p.sb("maxJP", [128, 512], F32)
    nmax = p.sb("nmax", [128, 1], F32)
    nrow = p.sb("nrow", [1, 512], F32)
    rsum = p.sb("rsum", [1, 512], F32)
    sstage = p.sb("sstage", [128, 512], F32)

    ps0 = p.ps("ps0", [128, 512], F32)
    ps1 = p.ps("ps1", [128, 512], F32)
    ps2 = p.ps("ps2", [128, 512], F32)
    ps3 = p.ps("ps3", [128, 512], F32)
    ps4 = p.ps("ps4", [128, 512], F32)
    ps5 = p.ps("ps5", [128, 512], F32)
    ps6 = p.ps("ps6", [128, 512], F32)

    p.dma("gpsimd", cosb[:, :], cosI, writes=["cosb"])
    p.dma("gpsimd", sinb[:, :], sinI, writes=["sinb"])
    p.dma("gpsimd", p128[:, :], P128, writes=["p128"])
    p.dma("gpsimd", oh[:, :], onesH, writes=["oh"])
    p.dma("gpsimd", oht[:, :], onesHT, writes=["oht"])
    p.dma("gpsimd", eye[:, :], eyef, writes=["eye"])
    p.dma("gpsimd", scol[:, :], s_col, writes=["scol"])
    p.memset(ones1[:, :], 1.0, writes=["ones1"])
    p.op("vector", lambda v: v.memset(onesk[:, :], 1.0), writes=["onesk"])
    p.act(ksc[:, :], scol[:, :], AF.Square, scale=math.sqrt(8192.0),
          reads=["scol"], writes=["ksc"])
    for kc in range(8):
        p.dma("gpsimd", xn_bf[:, kc * T:(kc + 1) * T],
              xnF[kc * 128:(kc + 1) * 128, :], writes=["xn_bf"],
              chan="gpsimd#x" + str(kc % 2))

    for which, wT, dest in (("q", wqT, q_bf), ("k", wkT, k_bf)):
        for kc in range(8):
            p.dma("gpsimd", w_bf[:, kc * C:(kc + 1) * C],
                  wT[kc * 128:(kc + 1) * 128, :], writes=["w_bf" + str(kc)],
                  chan="gpsimd#x" + str(kc % 2))
        for ct in range(8):
            for tc in range(4):
                tsl = slice(tc * 512, (tc + 1) * 512)
                for kc in range(8):
                    p.mm(ps0[:, :],
                         w_bf[:, kc * C + ct * 128: kc * C + (ct + 1) * 128],
                         xn_bf[:, kc * T + tc * 512: kc * T + (tc + 1) * 512],
                         kc == 0, kc == 7,
                         reads=["w_bf" + str(kc), "w_bf6", "w_bf7", "xn_bf"],
                         writes=["ps0"])
                p.copy("scalar", qraw[:, :], ps0[:, :], reads=["ps0"],
                       writes=["qraw"])
                p.mm(ps1[:, :], p128[:, :], qraw[:, :], True, True,
                     reads=["p128", "qraw"], writes=["ps1"])
                p.tt(t1[:, :], qraw[:, :], cosb[:, tsl], ALU.mult,
                     reads=["qraw", "cosb"], writes=["t1"])
                p.tt(t2[:, :], ps1[:, :], sinb[:, tsl], ALU.mult,
                     reads=["ps1", "sinb"], writes=["t2"])
                p.tt(qr[:, :], t1[:, :], t2[:, :], ALU.add,
                     reads=["t1", "t2"], writes=["qr"])
                p.act(sqv[:, :], qr[:, :], AF.Square, reads=["qr"],
                      writes=["sqv"])
                p.mm(ps2[0:2, :], oh[:, :], sqv[:, :], True, True,
                     reads=["oh", "sqv"], writes=["ps2"])
                p.act(stv[:, :], ps2[0:2, :], AF.Sqrt, bias=1e-24,
                      reads=["ps2"], writes=["stv"])
                p.recip(rnv[:, :], stv[:, :], reads=["stv"], writes=["rnv"])
                p.mm(ps3[:, :], oht[:, :], rnv[:, :], True, True,
                     reads=["oht", "rnv"], writes=["ps3"])
                dsl = slice(ct * T + tc * 512, ct * T + (tc + 1) * 512)
                if which == "q":
                    p.tt(dest[:, dsl], qr[:, :], ps3[:, :], ALU.mult,
                         reads=["qr", "ps3"], writes=["q_bf"])
                else:
                    p.stt(dest[:, dsl], qr[:, :], ksc[:, ct:ct + 1], ps3[:, :],
                          ALU.mult, ALU.mult, reads=["qr", "ksc", "ps3"],
                          writes=["k_bf"])

    for kc in range(8):
        p.dma("gpsimd", w_bf[:, kc * C:(kc + 1) * C],
              wvT[kc * 128:(kc + 1) * 128, :], writes=["w_bf" + str(kc)],
              chan="gpsimd#x" + str(kc % 2))
    for tt in range(16):
        for ic in range(2):
            for kc in range(8):
                p.mm(ps0[:, :],
                     xn_bf[:, kc * T + tt * 128: kc * T + (tt + 1) * 128],
                     w_bf[:, kc * C + ic * 512: kc * C + (ic + 1) * 512],
                     kc == 0, kc == 7,
                     reads=["xn_bf", "w_bf" + str(kc), "w_bf6", "w_bf7"],
                     writes=["ps0"])
            p.copy("scalar", v_bf[:, tt * C + ic * 512: tt * C + (ic + 1) * 512],
                   ps0[:, :], reads=["ps0"], writes=["v_bf"])

    P2R = [(ps0, "ps0"), (ps1, "ps1"), (ps5, "ps5"), (ps6, "ps6")]
    for h in range(16):
        cth = h // 2
        bp = (h % 2) * 64
        hsl = slice(bp, bp + 64)
        for qc in range(4):
            qsl = slice(cth * T + qc * 512, cth * T + (qc + 1) * 512)
            for kt in range(16):
                psA, key = P2R[kt % 4]
                p.mm(psA[:, :],
                     k_bf[hsl, cth * T + kt * 128: cth * T + (kt + 1) * 128],
                     q_bf[hsl, qsl], True, True, tp=(bp, 0),
                     reads=["k_bf", "q_bf"], writes=[key])
                if kt == 0:
                    p.copy("vector", maxJP[:, :], psA[:, :], reads=[key],
                           writes=["maxJP"])
                else:
                    p.tt(maxJP[:, :], maxJP[:, :], psA[:, :], ALU.max,
                         reads=["maxJP", key], writes=["maxJP"])
            for j in range(4):
                p.trans(ps2[:, 0:128], maxJP[:, j * 128:(j + 1) * 128],
                        eye[:, :], reads=["maxJP", "eye"], writes=["ps2"])
                p.red(nmax[:, :], ps2[:, 0:128], ALU.max, negate=True,
                      reads=["ps2"], writes=["nmax"])
                p.trans(ps3[0:1, 0:128], nmax[:, 0:1], eye[:, :],
                        reads=["nmax", "eye"], writes=["ps3"])
                p.copy("vector", nrow[:, j * 128:(j + 1) * 128],
                       ps3[0:1, 0:128], reads=["ps3"], writes=["nrow"])
            p.mm(ps3[:, :], ones1[:, :], nrow[:, :], True, True,
                 reads=["ones1", "nrow"], writes=["ps3"])
            p.copy("vector", nmaxb[:, :], ps3[:, :], reads=["ps3"],
                   writes=["nmaxb"])
            for kt in range(16):
                psA, key = P2R[kt % 4]
                p.mm(psA[:, :],
                     k_bf[hsl, cth * T + kt * 128: cth * T + (kt + 1) * 128],
                     q_bf[hsl, qsl], True, True, tp=(bp, 0),
                     reads=["k_bf", "q_bf"], writes=[key])
                ts_t, ts_k = (tsum0, "tsum0") if kt % 2 == 0 else (tsum1, "tsum1")
                p.tt(ts_t[:, :], psA[:, :], nmaxb[:, :], ALU.add,
                     reads=[key, "nmaxb"], writes=[ts_k])
                p.act(exp_bf[:, kt * 512:(kt + 1) * 512], ts_t[:, :], AF.Exp,
                      reads=[ts_k], writes=["exp_bf%d" % kt])
            for kt in range(16):
                p.mm(ps2[0:1, :], onesk[:, :],
                     exp_bf[:, kt * 512:(kt + 1) * 512], kt == 0, kt == 15,
                     reads=["onesk", "exp_bf%d" % kt], writes=["ps2"])
                p.mm(ps4[0:64, :],
                     v_bf[:, kt * C + h * 64: kt * C + (h + 1) * 64],
                     exp_bf[:, kt * 512:(kt + 1) * 512], kt == 0, kt == 15,
                     reads=["v_bf", "exp_bf%d" % kt], writes=["ps4"])
            p.recip(rsum[:, :], ps2[0:1, :], reads=["ps2"], writes=["rsum"])
            p.mm(ps3[:, :], ones1[:, :], rsum[:, :], True, True,
                 reads=["ones1", "rsum"], writes=["ps3"])
            p.copy("vector", qr[:, :], ps3[:, :], reads=["ps3"],
                   writes=["qr"])
            p.tt(xn_bf[bp:bp + 64, qsl], ps4[0:64, :], qr[0:64, :],
                 ALU.mult, reads=["ps4", "qr"], writes=["xn_bf"])

    for kc in range(8):
        p.dma("gpsimd", w_bf[:, kc * C:(kc + 1) * C],
              woT[kc * 128:(kc + 1) * 128, :], writes=["w_bf" + str(kc)],
              chan="gpsimd#x" + str(kc % 2))
    for ct in range(8):
        for tc in range(4):
            for kc in range(8):
                p.mm(ps0[:, :],
                     w_bf[:, kc * C + ct * 128: kc * C + (ct + 1) * 128],
                     xn_bf[:, kc * T + tc * 512: kc * T + (tc + 1) * 512],
                     kc == 0, kc == 7,
                     reads=["w_bf" + str(kc), "w_bf6", "w_bf7", "xn_bf"],
                     writes=["ps0"])
            p.copy("scalar", sstage[:, :], ps0[:, :], reads=["ps0"],
                   writes=["sstage"])
            p.dma("sync", sF_o[ct * 128:(ct + 1) * 128,
                               tc * 512:(tc + 1) * 512],
                  sstage[:, :], reads=["sstage"])


def emit_moe(p, l1_mode, xinF, W1, B1, W2, B2, gF_o, mgT_src=None,
             agout=None, probsT_mine=None, blk16=None, blk16T=None,
             mg_scratch=None):
    xin_bf = p.sb("xin_bf", [128, 8 * T], BF16)
    y_acc = p.sb("y_acc", [128, 8 * T], F32)
    w1a = p.sb("w1a", [128, 8 * 512], BF16)
    w1b = p.sb("w1b", [128, 8 * 512], BF16)
    w2a = p.sb("w2a", [128, 4 * C], BF16)
    w2b = p.sb("w2b", [128, 4 * C], BF16)
    b1s = p.sb("b1s", [128, 64], F32)
    b2s = p.sb("b2s", [8, C], F32)
    mgs = p.sb("mgs", [8, T], F32)
    mgrow = p.sb("mgrow", [1, 512], F32)
    mgball = p.sb("mgball", [128, 4 * 512], F32)
    hrelu = p.sb("hrelu", [128, 512], F32)
    hrelu2 = p.sb("hrelu2", [128, 512], F32)
    h_bf = p.sb("h_bf", [128, 4 * 512], BF16)
    h_bf2 = p.sb("h_bf2", [128, 4 * 512], BF16)
    ones1 = p.sb("ones1", [1, 128], F32)
    ps_h0 = p.ps("ps_h0", [128, 512], F32)
    ps_h1 = p.ps("ps_h1", [128, 512], F32)
    ps_y0 = p.ps("ps_y0", [128, 512], F32)
    ps_y1 = p.ps("ps_y1", [128, 512], F32)
    ps_m = p.ps("ps_m", [128, 512], F32)

    p.memset(ones1[:, :], 1.0, writes=["ones1"])
    for kc in range(8):
        p.dma("gpsimd", xin_bf[:, kc * T:(kc + 1) * T],
              xinF[kc * 128:(kc + 1) * 128, :], writes=["xin_bf"])
    p.dma("gpsimd", b2s[:, :], B2, writes=["b2s"])

    if l1_mode:
        pAll = p.sb("pAlls", [128, 1024], F32)
        msel = p.sb("msel", [8, T], F32)
        agout_rows = agout.rearrange("p (c t) -> (p c) t", c=2)
        lo = p.sb("lo", [128, 1], F32)
        hi = p.sb("hi", [128, 1], F32)
        mid = p.sb("mid", [128, 1], F32)
        msk = p.sb("msk", [128, 1024], F32)
        rowc = p.sb("rowc", [128, 1], F32)
        pred8 = p.sb("pred8", [8, 1], F32)
        predb = p.sb("predb", [128, 1], F32)
        d1 = p.sb("d1", [128, 1], F32)
        d2 = p.sb("d2", [128, 1], F32)
        th8 = p.sb("th8", [8, 1], F32)
        b16 = p.sb("b16", [128, 8], F32)
        b16t = p.sb("b16t", [8, 128], F32)
        gsum = p.sb("gsum", [1, T], F32)
        rgs = p.sb("rgs", [1, T], F32)
        rgsb = p.sb("rgsb", [128, 512], F32)
        ps_c = p.ps("ps_c", [8, 1], F32)
        ps_p = p.ps("ps_p", [128, 1], F32)
        ps_g = p.ps("ps_g", [1, 512], F32)

        p.dma("gpsimd", pAll[:, :], agout_rows, writes=["pAlls"])
        p.dma("gpsimd", mgs[:, :], probsT_mine, writes=["mgs"])
        p.dma("gpsimd", b16[:, :], blk16, writes=["b16"])
        p.dma("gpsimd", b16t[:, :], blk16T, writes=["b16t"])
        p.memset(lo[:, :], 0.0, writes=["lo"])
        p.memset(hi[:, :], 1.0, writes=["hi"])
        for _ in range(40):
            p.tt(mid[:, :], lo[:, :], hi[:, :], ALU.add, reads=["lo", "hi"],
                 writes=["mid"])
            p.ts(mid[:, :], mid[:, :], 0.5, None, ALU.mult, reads=["mid"],
                 writes=["mid"])
            p.ts(msk[:, :], pAll[:, :], mid[:, 0:1], None, ALU.is_ge,
                 reads=["pAlls", "mid"], writes=["msk"])
            p.red(rowc[:, :], msk[:, :], ALU.add, reads=["msk"],
                  writes=["rowc"])
            p.mm(ps_c[:, :], b16[:, :], rowc[:, :], True, True,
                 reads=["b16", "rowc"], writes=["ps_c"])
            p.ts(pred8[:, :], ps_c[:, :], 2048.0, None, ALU.is_ge,
                 reads=["ps_c"], writes=["pred8"])
            p.mm(ps_p[:, :], b16t[:, :], pred8[:, :], True, True,
                 reads=["b16t", "pred8"], writes=["ps_p"])
            p.copy("vector", predb[:, :], ps_p[:, :], reads=["ps_p"],
                   writes=["predb"])
            p.tt(d1[:, :], mid[:, :], lo[:, :], ALU.subtract,
                 reads=["mid", "lo"], writes=["d1"])
            p.tt(d2[:, :], hi[:, :], mid[:, :], ALU.subtract,
                 reads=["hi", "mid"], writes=["d2"])
            p.stt(lo[:, :], predb[:, :], d1[:, 0:1], lo[:, :], ALU.mult,
                  ALU.add, reads=["predb", "d1", "lo"], writes=["lo"])
            p.stt(hi[:, :], predb[:, :], d2[:, 0:1], mid[:, :], ALU.mult,
                  ALU.add, reads=["predb", "d2", "mid"], writes=["hi"])
        p.ts(b16[:, :], b16[:, :], 1.0 / 16.0, None, ALU.mult,
             reads=["b16"], writes=["b16"])
        p.mm(ps_c[:, :], b16[:, :], lo[:, :], True, True,
             reads=["b16", "lo"], writes=["ps_c"])
        p.copy("vector", th8[:, :], ps_c[:, :], reads=["ps_c"], writes=["th8"])
        p.ts(msel[:, :], mgs[:, :], th8[:, 0:1], None, ALU.is_ge,
             reads=["mgs", "th8"], writes=["msel"])
        p.tt(mgs[:, :], msel[:, :], mgs[:, :], ALU.mult,
             reads=["msel", "mgs"], writes=["mgs"])
        p.op("vector", lambda v: v.memset(d1[:, :], 1.0), writes=["d1"])
        for j in range(4):
            jsl = slice(j * 512, (j + 1) * 512)
            p.mm(ps_g[:, :], d1[0:8, 0:1], mgs[:, jsl], True, True,
                 reads=["d1", "mgs"], writes=["ps_g"])
            p.copy("vector", gsum[:, jsl], ps_g[:, :], reads=["ps_g"],
                   writes=["gsum"])
        p.ts(gsum[:, :], gsum[:, :], 1e-9, None, ALU.max, reads=["gsum"],
             writes=["gsum"])
        p.recip(rgs[:, :], gsum[:, :], reads=["gsum"], writes=["rgs"])
        p.dma("sync", mg_scratch, mgs[:, :], reads=["mgs"],
              writes=["mgs_dram"])
        mg_rows = mg_scratch
    else:
        p.dma("gpsimd", mgs[:, :], mgT_src, writes=["mgs"])
        mg_rows = mgT_src

    p.memset(y_acc[:, :], 0.0, writes=["y_acc"])
    actf = AF.Gelu if l1_mode else AF.Relu

    w1bufs = [(w1a, "w1a", "gpsimd#wa"), (w1b, "w1b", "gpsimd#wb")]
    w2bufs = [(w2a, "w2a", "gpsimd#w2a"), (w2b, "w2b", "gpsimd#w2b")]
    chunks = [(e, fc) for e in range(8) for fc in range(8)]

    def load_chunk(i):
        e, fc = chunks[i]
        w1t, w1k, ch1 = w1bufs[i % 2]
        w2t, w2k, ch2 = w2bufs[i % 2]
        for kc in range(8):
            p.dma("gpsimd", w1t[:, kc * 512:(kc + 1) * 512],
                  W1[e, kc * 128:(kc + 1) * 128, fc * 512:(fc + 1) * 512],
                  writes=[w1k + str(kc)], chan=ch1)
        for fs in range(4):
            p.dma("gpsimd", w2t[:, fs * C:(fs + 1) * C],
                  W2[e, fc * 512 + fs * 128: fc * 512 + (fs + 1) * 128, :],
                  writes=[w2k + str(fs)], chan=ch2)

    load_chunk(0)
    h_bufs = [(h_bf, "h_bf"), (h_bf2, "h_bf2")]
    hre_bufs = [(hrelu, "hrelu"), (hrelu2, "hrelu2")]

    def emit_fc1(i, tc, jpar):
        e, fc = chunks[i]
        w1t, w1k, _ = w1bufs[i % 2]
        ht, hk = h_bufs[jpar]
        b1k = "b1s" + str(e % 2)
        b1off = (e % 2) * 32
        tslk = slice(tc * 512, (tc + 1) * 512)
        for fs in range(4):
            psh, pshk = (ps_h0, "ps_h0") if fs % 2 == 0 else (ps_h1, "ps_h1")
            for kc in range(8):
                p.mm(psh[:, :],
                     w1t[:, kc * 512 + fs * 128: kc * 512 + (fs + 1) * 128],
                     xin_bf[:, kc * T + tc * 512: kc * T + (tc + 1) * 512],
                     kc == 0, kc == 7,
                     reads=[w1k + str(kc), w1k + "7", "xin_bf"],
                     writes=[pshk])
            hr, hrk = hre_bufs[fs % 2]
            p.act(hr[:, :], psh[:, :], actf,
                  bias=b1s[:, b1off + fc * 4 + fs: b1off + fc * 4 + fs + 1],
                  reads=[pshk, b1k], writes=[hrk])
            p.tt(ht[:, fs * 512:(fs + 1) * 512], hr[:, :],
                 mgball[:, tslk], ALU.mult, reads=[hrk, "mgball"],
                 writes=[hk])

    def emit_fc2(i, tc, jpar):
        w2t, w2k, _ = w2bufs[i % 2]
        ht, hk = h_bufs[jpar]
        for dt in range(8):
            psy, psyk = (ps_y0, "ps_y0") if dt % 2 == 0 else (ps_y1, "ps_y1")
            for fs in range(4):
                p.mm(psy[:, :],
                     w2t[:, fs * C + dt * 128: fs * C + (dt + 1) * 128],
                     ht[:, fs * 512:(fs + 1) * 512],
                     fs == 0, fs == 3,
                     reads=[w2k + str(fs), w2k + "3", hk],
                     writes=[psyk])
            ysl = slice(dt * T + tc * 512, dt * T + (tc + 1) * 512)
            p.tt(y_acc[:, ysl], y_acc[:, ysl], psy[:, :], ALU.add,
                 reads=["y_acc", psyk], writes=["y_acc"])

    prev = None
    for j, (i, tc) in enumerate([(i, tc) for i in range(len(chunks))
                                 for tc in range(4)]):
        e, fc = chunks[i]
        if tc == 1 and i + 1 < len(chunks):
            load_chunk(i + 1)
        if tc == 0:
            if fc == 0:
                p.dma("gpsimd", b1s[:, (e % 2) * 32:(e % 2) * 32 + 32],
                      B1[e, :, :], writes=["b1s" + str(e % 2)])
                for tcb in range(4):
                    p.dma("gpsimd", mgrow[:, :],
                          mg_rows[e:e + 1, tcb * 512:(tcb + 1) * 512],
                          reads=["mgs_dram"], writes=["mgrow"])
                    p.mm(ps_m[:, :], ones1[:, :], mgrow[:, :], True, True,
                         reads=["ones1", "mgrow"], writes=["ps_m"])
                    p.copy("vector", mgball[:, tcb * 512:(tcb + 1) * 512],
                           ps_m[:, :], reads=["ps_m"], writes=["mgball"])
        emit_fc1(i, tc, j % 2)
        if prev is not None:
            emit_fc2(*prev)
        prev = (i, tc, j % 2)
    emit_fc2(*prev)

    for tc in range(4):
        tsl = slice(tc * 512, (tc + 1) * 512)
        if l1_mode:
            p.mm(ps_m[:, :], ones1[:, :], rgs[0:1, tsl], True, True,
                 reads=["ones1", "rgs"], writes=["ps_m"])
            p.copy("vector", rgsb[:, :], ps_m[:, :], reads=["ps_m"],
                   writes=["rgsb"])
        for dt in range(8):
            ysl = slice(dt * T + tc * 512, dt * T + (tc + 1) * 512)
            p.mm(ps_y0[:, :], b2s[:, dt * 128:(dt + 1) * 128], mgs[:, tsl],
                 True, True, reads=["b2s", "mgs"], writes=["ps_y0"])
            p.tt(y_acc[:, ysl], y_acc[:, ysl], ps_y0[:, :], ALU.add,
                 reads=["y_acc", "ps_y0"], writes=["y_acc"])
            if l1_mode:
                p.tt(y_acc[:, ysl], y_acc[:, ysl], rgsb[:, :], ALU.mult,
                     reads=["y_acc", "rgsb"], writes=["y_acc"])
    for dt in range(8):
        p.dma("sync", gF_o[dt * 128:(dt + 1) * 128, :],
              y_acc[:, dt * T:(dt + 1) * T], reads=["y_acc"])


def emit_combine(p, xF, sF, gF, eyef, oF=None, oTM=None):
    xt = p.sb("xt", [128, T], F32)
    st_ = p.sb("st_", [128, T], F32)
    gt = p.sb("gt", [128, T], F32)
    sg = p.sb("sg", [128, T], F32)
    ot = p.sb("ot", [128, T], F32)
    eye = p.sb("eye", [128, 128], F32)
    stg0 = p.sb("stg0", [128, 2048], F32)
    stg1 = p.sb("stg1", [128, 2048], F32)
    ps_tr = p.ps("ps_tr", [128, 512], F32)
    ps_tr2 = p.ps("ps_tr2", [128, 512], F32)
    p.dma("gpsimd", eye[:, :], eyef, writes=["eye"])
    for ct in range(8):
        rows = slice(ct * 128, (ct + 1) * 128)
        ch = "gpsimd#c" + str(ct % 2)
        p.dma("gpsimd", xt[:, :], xF[rows, :], writes=["xt"], chan=ch)
        p.dma("gpsimd", st_[:, :], sF[rows, :], writes=["st_"], chan=ch)
        p.dma("gpsimd", gt[:, :], gF[rows, :], writes=["gt"], chan=ch)
        p.act(sg[:, :], gt[:, :], AF.Silu, reads=["gt"], writes=["sg"])
        p.tt(sg[:, :], st_[:, :], sg[:, :], ALU.mult, reads=["st_", "sg"],
             writes=["sg"])
        p.tt(ot[:, :], xt[:, :], sg[:, :], ALU.add, reads=["xt", "sg"],
             writes=["ot"])
        if oF is not None:
            p.dma("sync", oF[rows, :], ot[:, :], reads=["ot"])
        if oTM is not None:
            stg, sk = (stg0, "stg0") if ct % 2 == 0 else (stg1, "stg1")
            for tt in range(16):
                pst, psk = (ps_tr, "ps_tr") if tt % 2 == 0 else (ps_tr2, "ps_tr2")
                p.trans(pst[:, 0:128], ot[:, tt * 128:(tt + 1) * 128],
                        eye[:, :], reads=["ot", "eye"], writes=[psk])
                p.copy("vector", stg[:, tt * 128:(tt + 1) * 128],
                       pst[:, 0:128], reads=[psk], writes=[sk])
            dst = bass.AP(oTM.tensor, ct * 128, [[C, 128], [128 * C, 16], [1, 128]])
            srcv = bass.AP(stg, 0, [[2048, 128], [128, 16], [1, 128]])
            p.dma("sync", dst, srcv, reads=[sk], chan="sync#t" + str(ct % 2))


def emit_attn1(p, xnF, wqT, wkT, wvT, wgT, woT, gnw, cosH, sinH, lamq_d,
               lamk_d, mdel, s1F_o):
    xn_bf = p.sb("xn_bf", [128, 8 * T], BF16)
    w_bf = p.sb("w_bf", [128, 8 * C], BF16)
    q_bf = p.sb("q_bf", [128, 8 * T], BF16)
    k_bf = p.sb("k_bf", [128, 8 * T], BF16)
    v_sb = p.sb("v_sb", [128, 16 * 512], BF16)
    sg_bf = p.sb("sg_bf", [128, 4 * T], BF16)
    wh = p.sb("wh", [128, 8 * 512], BF16)
    woh = p.sb("woh", [128, 4 * C], BF16)
    cosb = p.sb("cosb", [128, T], F32)
    sinb = p.sb("sinb", [128, T], F32)
    gnws = p.sb("gnws", [128, 4], F32)
    masks = p.sb("masks", [128, 4 * 512], F32)
    ones1 = p.sb("ones1", [1, 128], F32)
    onesc = p.sb("onesc", [128, 1], F32)
    q1 = p.sb("q1", [128, 512], F32)
    q2 = p.sb("q2", [128, 512], F32)
    t1 = p.sb("t1", [128, 512], F32)
    t2 = p.sb("t2", [128, 512], F32)
    t3 = p.sb("t3", [128, 512], F32)
    lamb = p.sb("lamb", [128, 512], F32)
    lamb2 = p.sb("lamb2", [128, 512], F32)
    s_bf = p.sb("s_bf", [128, 512], BF16)
    s_bf2 = p.sb("s_bf2", [128, 512], BF16)
    s_bf3 = p.sb("s_bf3", [128, 512], BF16)
    srow = p.sb("srow", [1, 512], F32)
    rr = p.sb("rr", [1, 512], F32)
    rb = p.sb("rb", [128, 512], F32)

    ps_s = p.ps("ps_s", [128, 512], F32)
    ps_s2 = p.ps("ps_s2", [128, 512], F32)
    ps_o = p.ps("ps_o", [128, 2048], F32)
    ps_x = p.ps("ps_x", [128, 512], F32)
    ps_r = p.ps("ps_r", [1, 512], F32)

    p.dma("gpsimd", cosb[:, :], cosH, writes=["cosb"])
    p.dma("gpsimd", sinb[:, :], sinH, writes=["sinb"])
    p.dma("gpsimd", gnws[:, :], gnw, writes=["gnws"])
    for j in range(4):
        p.dma("gpsimd", masks[:, j * 512:(j + 1) * 512], mdel[j, :, :],
              writes=["masks"])
    p.memset(ones1[:, :], 1.0, writes=["ones1"])
    p.op("vector", lambda v: v.memset(onesc[:, :], 1.0), writes=["onesc"])
    for kc in range(8):
        p.dma("gpsimd", xn_bf[:, kc * T:(kc + 1) * T],
              xnF[kc * 128:(kc + 1) * 128, :], writes=["xn_bf"],
              chan="gpsimd#x" + str(kc % 2))

    for which, wT, dest, lam in (("q", wqT, q_bf, lamq_d),
                                 ("k", wkT, k_bf, lamk_d)):
        destkey = "q_bf" if which == "q" else "k_bf"
        for kc in range(8):
            p.dma("gpsimd", w_bf[:, kc * C:(kc + 1) * C],
                  wT[kc * 128:(kc + 1) * 128, :], writes=["w_bf"],
                  chan="gpsimd#x" + str(kc % 2))
        srows = [(srow, "srow"), (rr, "rr")]
        lambs = [(lamb, "lamb"), (lamb2, "lamb2")]
        qds = [(q1, "q1"), (q2, "q2"), (t3, "t3"), (rb, "rb")]
        for it, (h, tc) in enumerate([(h, tc) for h in range(4)
                                      for tc in range(4)]):
            tsl = slice(tc * 512, (tc + 1) * 512)
            sr, srk = srows[it % 2]
            lm, lmk = lambs[it % 2]
            p.dma("gpsimd", sr[:, :],
                  lam[0:1, h * T + tc * 512: h * T + (tc + 1) * 512],
                  writes=[srk], chan="gpsimd#x" + str(it % 2))
            qa, qak = qds[(it % 2) * 2]
            qb, qbk = qds[(it % 2) * 2 + 1]
            for half, dst, dkey in ((0, qa, qak), (1, qb, qbk)):
                ct = 2 * h + half
                psA, keyA = (ps_s, "ps_s") if half == 0 else (ps_s2, "ps_s2")
                for kc in range(8):
                    p.mm(psA[:, :],
                         w_bf[:, kc * C + ct * 128: kc * C + (ct + 1) * 128],
                         xn_bf[:, kc * T + tc * 512: kc * T + (tc + 1) * 512],
                         kc == 0, kc == 7, reads=["w_bf", "xn_bf"],
                         writes=[keyA])
                p.copy("scalar", dst[:, :], psA[:, :], reads=[keyA],
                       writes=[dkey])
            psB = ps_x[:, :] if it % 2 == 0 else ps_o[:, 0:512]
            psBk = "ps_x" if it % 2 == 0 else "ps_o"
            p.mm(psB, ones1[:, :], sr[:, :], True, True,
                 reads=["ones1", srk], writes=[psBk])
            p.copy("vector", lm[:, :], psB, reads=[psBk], writes=[lmk])
            p.tt(t1[:, :], qa[:, :], cosb[:, tsl], ALU.mult,
                 reads=[qak, "cosb"], writes=["t1"])
            p.tt(t2[:, :], qb[:, :], sinb[:, tsl], ALU.mult,
                 reads=[qbk, "sinb"], writes=["t2"])
            p.tt(t1[:, :], t1[:, :], t2[:, :], ALU.subtract,
                 reads=["t1", "t2"], writes=["t1"])
            d1sl = slice(2 * h * T + tc * 512, 2 * h * T + (tc + 1) * 512)
            p.tt(dest[:, d1sl], t1[:, :], lm[:, :], ALU.mult,
                 reads=["t1", lmk], writes=[destkey])
            p.tt(t1[:, :], qa[:, :], sinb[:, tsl], ALU.mult,
                 reads=[qak, "sinb"], writes=["t1"])
            p.tt(t2[:, :], qb[:, :], cosb[:, tsl], ALU.mult,
                 reads=[qbk, "cosb"], writes=["t2"])
            p.tt(t1[:, :], t1[:, :], t2[:, :], ALU.add,
                 reads=["t1", "t2"], writes=["t1"])
            d2sl = slice((2 * h + 1) * T + tc * 512,
                         (2 * h + 1) * T + (tc + 1) * 512)
            p.tt(dest[:, d2sl], t1[:, :], lm[:, :], ALU.mult,
                 reads=["t1", lmk], writes=[destkey])

    for h in range(4):
        for kc in range(8):
            p.dma("gpsimd", wh[:, kc * 512:(kc + 1) * 512],
                  wvT[kc * 128:(kc + 1) * 128, h * 512:(h + 1) * 512],
                  writes=["wh"], chan="gpsimd#x" + str(kc % 2))
        for tt in range(16):
            psA, keyA = (ps_s, "ps_s") if tt % 2 == 0 else (ps_s2, "ps_s2")
            for kc in range(8):
                p.mm(psA[:, :],
                     xn_bf[:, kc * T + tt * 128: kc * T + (tt + 1) * 128],
                     wh[:, kc * 512:(kc + 1) * 512], kc == 0, kc == 7,
                     reads=["xn_bf", "wh"], writes=[keyA])
            p.copy("scalar", v_sb[:, tt * 512:(tt + 1) * 512], psA[:, :],
                   reads=[keyA], writes=["v_sb"])
        for kc in range(8):
            p.dma("gpsimd", wh[:, kc * 512:(kc + 1) * 512],
                  wgT[kc * 128:(kc + 1) * 128, h * 512:(h + 1) * 512],
                  writes=["wh"], chan="gpsimd#x" + str(kc % 2))
        for gi, (gt, tc) in enumerate([(gt, tc) for gt in range(4)
                                       for tc in range(4)]):
            psA, keyA = (ps_s, "ps_s") if gi % 2 == 0 else (ps_s2, "ps_s2")
            for kc in range(8):
                p.mm(psA[:, :],
                     wh[:, kc * 512 + gt * 128: kc * 512 + (gt + 1) * 128],
                     xn_bf[:, kc * T + tc * 512: kc * T + (tc + 1) * 512],
                     kc == 0, kc == 7, reads=["wh", "xn_bf"],
                     writes=[keyA])
            p.act(sg_bf[:, gt * T + tc * 512: gt * T + (tc + 1) * 512],
                  psA[:, :], AF.Silu, reads=[keyA], writes=["sg_bf"])
        sbr = [(s_bf, "s_bf"), (s_bf2, "s_bf2"), (s_bf3, "s_bf3")]
        for qc in range(4):
            qsl0 = 2 * h * T + qc * 512
            qsl1 = (2 * h + 1) * T + qc * 512
            kt_last = qc * 4 + 3

            def sc_emit(kt, qsl0=qsl0, qsl1=qsl1, qc=qc):
                psA, keyA = (ps_s, "ps_s") if kt % 2 == 0 else (ps_s2, "ps_s2")
                p.mm(psA[:, :],
                     k_bf[:, 2 * h * T + kt * 128: 2 * h * T + (kt + 1) * 128],
                     q_bf[:, qsl0:qsl0 + 512], True, False,
                     reads=["k_bf", "q_bf"], writes=[keyA])
                p.mm(psA[:, :],
                     k_bf[:, (2 * h + 1) * T + kt * 128:
                          (2 * h + 1) * T + (kt + 1) * 128],
                     q_bf[:, qsl1:qsl1 + 512], False, True,
                     reads=["k_bf", "q_bf"], writes=[keyA])
                delta = qc * 512 - kt * 128
                sb_t, sb_k = sbr[kt % 3]
                if delta >= 128:
                    p.copy("scalar", sb_t[:, :], psA[:, :], reads=[keyA],
                           writes=[sb_k])
                else:
                    j = (delta + 384) // 128
                    p.tt(sb_t[:, :], psA[:, :], masks[:, j * 512:(j + 1) * 512],
                         ALU.mult, reads=[keyA, "masks"], writes=[sb_k])

            def sv_emit(kt, kt_last=kt_last):
                sb_t, sb_k = sbr[kt % 3]
                for dvt in range(4):
                    p.mm(ps_o[:, dvt * 512:(dvt + 1) * 512],
                         v_sb[:, kt * 512 + dvt * 128: kt * 512 + (dvt + 1) * 128],
                         sb_t[:, :], kt == 0, kt == kt_last,
                         reads=["v_sb", sb_k], writes=["ps_o"])

            pend = []
            for kt in range(kt_last + 1):
                sc_emit(kt)
                pend.append(kt)
                if kt >= 1:
                    sv_emit(pend.pop(0))
            for kt in pend:
                sv_emit(kt)
            for dvt in range(4):
                p.act(t2[:, :], ps_o[:, dvt * 512:(dvt + 1) * 512], AF.Square,
                      reads=["ps_o"], writes=["t2"])
                p.mm(ps_r[:, :], onesc[:, :], t2[:, :], dvt == 0, dvt == 3,
                     reads=["onesc", "t2"], writes=["ps_r"])
            p.copy("vector", srow[:, :], ps_r[:, :], reads=["ps_r"],
                   writes=["srow"])
            p.act(srow[:, :], srow[:, :], AF.Sqrt, bias=1e-5, scale=1.0 / 512,
                  reads=["srow"], writes=["srow"])
            p.recip(rr[:, :], srow[:, :], reads=["srow"], writes=["rr"])
            p.mm(ps_x[:, :], ones1[:, :], rr[:, :], True, True,
                 reads=["ones1", "rr"], writes=["ps_x"])
            p.copy("vector", rb[:, :], ps_x[:, :], reads=["ps_x"],
                   writes=["rb"])
            for dvt in range(4):
                p.stt(s_bf[:, :], ps_o[:, dvt * 512:(dvt + 1) * 512],
                      gnws[:, dvt:dvt + 1], rb[:, :], ALU.mult, ALU.mult,
                      reads=["ps_o", "gnws", "rb"], writes=["s_bf"])
                osl = slice(dvt * T + qc * 512, dvt * T + (qc + 1) * 512)
                p.tt(w_bf[:, osl], s_bf[:, :], sg_bf[:, osl], ALU.mult,
                     reads=["s_bf", "sg_bf"], writes=["w_bf"])
        for fs in range(4):
            p.dma("gpsimd", woh[:, fs * C:(fs + 1) * C],
                  woT[h * 512 + fs * 128: h * 512 + (fs + 1) * 128, :],
                  writes=["woh"], chan="gpsimd#x" + str(fs % 2))
        for oi, (ct, tc) in enumerate([(ct, tc) for ct in range(8)
                                       for tc in range(4)]):
            psA, keyA = (ps_s, "ps_s") if oi % 2 == 0 else (ps_s2, "ps_s2")
            stg, sgk = (t1, "t1") if oi % 2 == 0 else (t3, "t3")
            for fs in range(4):
                p.mm(psA[:, :],
                     woh[:, fs * C + ct * 128: fs * C + (ct + 1) * 128],
                     w_bf[:, fs * T + tc * 512: fs * T + (tc + 1) * 512],
                     fs == 0, fs == 3, reads=["woh", "w_bf"],
                     writes=[keyA])
            p.copy("scalar", stg[:, :], psA[:, :], reads=[keyA],
                   writes=[sgk])
            dsl = (slice(ct * 128, (ct + 1) * 128),
                   slice(tc * 512, (tc + 1) * 512))
            okey = "s1o%d" % oi
            if h == 0:
                p.dma("gpsimd", s1F_o[dsl[0], dsl[1]], stg[:, :],
                      reads=[sgk], writes=[okey],
                      chan="gpsimd#o" + str(oi % 2))
            else:
                p.op("gpsimd",
                     lambda g, d0=dsl[0], d1=dsl[1], s=stg: g.dma_start(
                         s1F_o[d0, d1], s[:, :], accum_op=ALU.add),
                     reads=[sgk, okey], writes=[okey], dma=True,
                     chan="gpsimd#o" + str(oi % 2))



_cache = {}


def _ct(a):
    return np.ascontiguousarray(a, dtype=np.float32)


def _rope_tables_interleaved():
    d = np.arange(128) % 64
    inv = (1.0 / 10000.0 ** ((d // 2 * 2) / 64.0))[:, None]
    pos = np.arange(T)[None, :]
    fr = pos * inv
    return _ct(np.cos(fr)), _ct(np.sin(fr))


def _rope_tables_half():
    inv = (1.0 / 10000.0 ** (np.arange(0, 256, 2) / 256.0))[:, None]
    pos = np.arange(T)[None, :]
    fr = pos * inv
    return _ct(np.cos(fr)), _ct(np.sin(fr))


def _p128():
    P = np.zeros((128, 128), np.float32)
    for blk in range(2):
        for i in range(64):
            gi = blk * 64 + i
            if i % 2 == 0:
                P[blk * 64 + i + 1, gi] = -1.0
            else:
                P[blk * 64 + i - 1, gi] = 1.0
    return P


def build_fused():
    nc = bass.Bass("TRN2", target_bir_lowering=False, debug=False)
    EI = "ExternalInput"

    def di(name, shape):
        return nc.dram_tensor(name, list(shape), F32, kind=EI).ap()

    def dint(name, shape):
        return nc.dram_tensor(name, list(shape), F32, kind="Internal").ap()

    x = di("x", [T, C])
    xF_in = di("xF_in", [C, T])
    eyef = di("eyef", [128, 128])
    ln_g = di("ln_g", [1, C])
    ln_b = di("ln_b", [1, C])
    wq0T = di("wq0T", [C, C])
    wk0T = di("wk0T", [C, C])
    wv0T = di("wv0T", [C, C])
    wo0T = di("wo0T", [C, C])
    s_col = di("s_col", [128, 8])
    cosI = di("cosI", [128, T])
    sinI = di("sinI", [128, T])
    P128 = di("P128", [128, 128])
    onesH = di("onesH", [128, 2])
    onesHT = di("onesHT", [2, 128])
    Wr = di("Wr", [C, 8])
    br = di("br", [1, 8])
    W1 = di("W1", [8, C, 4096])
    B1 = di("B1", [8, 128, 32])
    W2 = di("W2", [8, 4096, C])
    B2 = di("B2", [8, C])
    norm_w = di("norm_w", [1, C])
    zeros_c = di("zeros_c", [1, C])
    r1WT = di("r1WT", [C, 8])
    r1b = di("r1b", [1, 8])
    wq1T = di("wq1T", [C, C])
    wk1T = di("wk1T", [C, C])
    wv1T = di("wv1T", [C, 2048])
    wg1T = di("wg1T", [C, 2048])
    wo1T = di("wo1T", [2048, C])
    gnw = di("gnw", [128, 4])
    cosH = di("cosH", [128, T])
    sinH = di("sinH", [128, T])
    lamq = di("lamq", [1, 4 * T])
    lamk = di("lamk", [1, 4 * T])
    mdel = di("mdel", [4, 128, 512])
    F1 = di("F1", [8, C, 4096])
    Fb1 = di("Fb1", [8, 128, 32])
    F2 = di("F2", [8, 4096, C])
    Fb2 = di("Fb2", [8, C])
    blk16 = di("blk16", [128, 8])
    blk16T = di("blk16T", [8, 128])

    xnF = dint("xnF", [C, T])
    xinF = dint("xinF", [C, T])
    mgT0 = dint("mgT0", [8, T])
    s0F = dint("s0F", [C, T])
    g0F = dint("g0F", [C, T])
    x1TM = dint("x1TM", [T, C])
    xn1F = dint("xn1F", [C, T])
    x1F = dint("x1F", [C, T])
    xin1F = dint("xin1F", [C, T])
    s1F = dint("s1F", [C, T])
    g1F = dint("g1F", [C, T])
    mg_scr = dint("mg_scr", [8, T])
    probsT_h = nc.dram_tensor("probsTb", [8, T], F32)
    pAll_h = nc.dram_tensor("pAllb", [64, T], F32)
    probsT1 = probsT_h.ap()
    pAll = pAll_h.ap()

    out = nc.dram_tensor("out", [T, C], F32, kind="ExternalOutput").ap()

    p = Prog("n0", nc=nc)
    emit_norm(p, "ln2", x, ln_g, ln_b, eyef, xnF, xinF)
    p.finish()
    nc.reset()

    p = Prog("r0", nc=nc)
    emit_router(p, xnF, Wr, br, eyef, mgT_o=mgT0)
    p.finish()
    nc.reset()

    p = Prog("a0", nc=nc)
    emit_attn0(p, xnF, wq0T, wk0T, wv0T, wo0T, s_col, cosI, sinI, P128,
               onesH, onesHT, eyef, s0F)
    p.finish()
    nc.reset()

    p = Prog("m0", nc=nc)
    emit_moe(p, False, xinF, W1, B1, W2, B2, g0F, mgT_src=mgT0)
    p.finish()
    nc.reset()

    p = Prog("c0", nc=nc)
    emit_combine(p, xF_in, s0F, g0F, eyef, oF=x1F, oTM=x1TM)
    p.finish()
    nc.reset()

    p = Prog("n1", nc=nc)
    emit_norm(p, "rms", x1TM, norm_w, zeros_c, eyef, xn1F, xin1F)
    p.finish()
    nc.reset()

    p = Prog("r1", nc=nc)
    emit_router(p, xin1F, r1WT, r1b, eyef, probsT_o=probsT1)
    p.finish()
    nc.reset()

    p = Prog("cc", nc=nc)
    p.op("gpsimd", lambda g: g.collective_compute(
        "AllGather", ALU.bypass, replica_groups=[list(range(NCORE))],
        ins=[probsT_h.ap().opt()], outs=[pAll_h.ap().opt()]),
        writes=["pAll"])
    p.finish()
    nc.reset()

    p = Prog("a1", nc=nc)
    emit_attn1(p, xn1F, wq1T, wk1T, wv1T, wg1T, wo1T, gnw, cosH, sinH,
               lamq, lamk, mdel, s1F)
    p.finish()
    nc.reset()

    p = Prog("m1", nc=nc)
    emit_moe(p, True, xin1F, F1, Fb1, F2, Fb2, g1F, agout=pAll,
             probsT_mine=probsT1, blk16=blk16, blk16T=blk16T,
             mg_scratch=mg_scr)
    p.finish()
    nc.reset()

    p = Prog("c1", nc=nc)
    emit_combine(p, x1F, s1F, g1F, eyef, oTM=out)
    p.finish()
    return nc


def kernel(**inp):
    x_full = np.asarray(inp["L0_attn_x"], np.float32)

    wq0T = _ct(inp["L0_attn_Wq"].T)
    wk0T = _ct(inp["L0_attn_Wk"].T)
    wv0T = _ct(inp["L0_attn_Wv"].T)
    wo0T = _ct(inp["L0_attn_Wo"].T)
    s_col = _ct(np.asarray(inp["L0_attn_s_qk"], np.float32).reshape(8, 128).T)
    cosI, sinI = _rope_tables_interleaved()
    P128v = _p128()
    onesH = np.zeros((128, 2), np.float32)
    onesH[:64, 0] = 1.0
    onesH[64:, 1] = 1.0
    onesHT = _ct(onesH.T)
    eyef = np.eye(128, dtype=np.float32)

    Wr = _ct(inp["L0_ffn_Wr"])
    br = _ct(np.asarray(inp["L0_ffn_br"]).reshape(1, 8))
    ln_g = _ct(np.asarray(inp["L0_ffn_ln_g"]).reshape(1, C))
    ln_b = _ct(np.asarray(inp["L0_ffn_ln_b"]).reshape(1, C))
    W1 = _ct(inp["L0_ffn_W1"])
    B1 = _ct(np.asarray(inp["L0_ffn_b1"]).reshape(8, 32, 128).transpose(0, 2, 1))
    W2 = _ct(inp["L0_ffn_W2"])
    B2 = _ct(inp["L0_ffn_b2"])

    wq1T = _ct(inp["L1_attn_Wq"].T)
    wk1T = _ct(inp["L1_attn_Wk"].T)
    wv1T = _ct(inp["L1_attn_Wv"].T)
    wg1T = _ct(inp["L1_attn_Wg"].T)
    wo1T = _ct(inp["L1_attn_Wo"].T)
    gnw = _ct(np.asarray(inp["L1_attn_g_norm_w"]).reshape(4, 128).T)
    cosH, sinH = _rope_tables_half()
    log_decay = np.log1p(-np.exp2(-5.0 - np.arange(4, dtype=np.float64)))
    tpos = np.arange(T, dtype=np.float64)
    lamq = _ct((np.exp(log_decay[:, None] * tpos[None, :])
                * (256.0 ** -0.5)).reshape(1, -1))
    lamk = _ct(np.exp(-log_decay[:, None] * tpos[None, :]).reshape(1, -1))
    mdel = np.zeros((4, 128, 512), np.float32)
    for j, delta in enumerate((-384, -256, -128, 0)):
        jj = np.arange(512)[None, :] + delta
        pp = np.arange(128)[:, None]
        mdel[j] = (jj >= pp).astype(np.float32)
    mdel = _ct(mdel)

    norm_w = _ct(np.asarray(inp["L1_ffn_norm_weight"]).reshape(1, C))
    zeros_c = np.zeros((1, C), np.float32)
    r1WT = _ct(inp["L1_ffn_router_w"].T)
    r1b = _ct(np.asarray(inp["L1_ffn_router_b"]).reshape(1, 8))
    F1 = _ct(np.transpose(inp["L1_ffn_fc1_w"], (0, 2, 1)))
    Fb1 = _ct(np.asarray(inp["L1_ffn_fc1_b"]).reshape(8, 32, 128).transpose(0, 2, 1))
    F2 = _ct(np.transpose(inp["L1_ffn_fc2_w"], (0, 2, 1)))
    Fb2 = _ct(inp["L1_ffn_fc2_b"])
    blk16 = np.zeros((128, 8), np.float32)
    for r in range(128):
        blk16[r, (r // 2) % 8] = 1.0
    blk16T = _ct(blk16.T)

    if "nc" not in _cache:
        _cache["nc"] = build_fused()
    nc = _cache["nc"]

    shared = dict(eyef=eyef, ln_g=ln_g, ln_b=ln_b, wq0T=wq0T, wk0T=wk0T,
                  wv0T=wv0T, wo0T=wo0T, s_col=s_col, cosI=cosI, sinI=sinI,
                  P128=P128v, onesH=onesH, onesHT=onesHT, Wr=Wr, br=br,
                  W1=W1, B1=B1, W2=W2, B2=B2, norm_w=norm_w, zeros_c=zeros_c,
                  r1WT=r1WT, r1b=r1b, wq1T=wq1T, wk1T=wk1T, wv1T=wv1T,
                  wg1T=wg1T, wo1T=wo1T, gnw=gnw, cosH=cosH, sinH=sinH,
                  lamq=lamq, lamk=lamk, mdel=mdel,
                  F1=F1, Fb1=Fb1, F2=F2, Fb2=Fb2, blk16=blk16, blk16T=blk16T)
    in_maps = []
    for c in range(NCORE):
        m = dict(shared)
        m["x"] = _ct(x_full[c])
        m["xF_in"] = _ct(x_full[c].T)
        in_maps.append(m)
    res = run_bass_kernel_spmd(nc, in_maps, core_ids=list(range(NCORE)))
    out = np.stack([np.asarray(r["out"]) for r in res.results], axis=0)
    return out.astype(np.float32)


# version 19
# speedup vs baseline: 1.1334x; 1.1334x over previous
import contextlib
import math

import numpy as np
import ml_dtypes

import concourse.bass as bass
import concourse.mybir as mybir
from concourse.bass_utils import run_bass_kernel_spmd

F32 = mybir.dt.float32
BF16 = mybir.dt.bfloat16
AF = mybir.ActivationFunctionType
ALU = mybir.AluOpType
AX = mybir.AxisListType

T = 2048
C = 1024
NCORE = 8
BF = ml_dtypes.bfloat16

ENGINES = ("tensor", "vector", "scalar", "gpsimd", "sync")


class Prog:

    def __init__(self, name, nc=None):
        self.nc = nc or bass.Bass("TRN2", target_bir_lowering=False, debug=False)
        self.es = contextlib.ExitStack()
        self.queues = {e: [] for e in ENGINES}
        self.sems = {}
        self.cnt = {e: 0 for e in ENGINES}
        self.writer = {}
        self.readers = {}
        self.name = name
        self._consts = {}
        self.hiwater = {}
        self.dma_chans = set()

    def _sem(self, e):
        if e not in self.sems:
            self.sems[e] = self.es.enter_context(
                self.nc.semaphore(self.name + "_s_" + e))
            self.cnt.setdefault(e, 0)
        return self.sems[e]

    def dram(self, name, shape, dtype, kind):
        return self.nc.dram_tensor(name, list(shape), dtype, kind=kind).ap()

    def sb(self, name, shape, dtype):
        return self.es.enter_context(
            self.nc.sbuf_tensor(self.name + "_" + name, list(shape), dtype))

    def ps(self, name, shape, dtype=F32):
        return self.es.enter_context(
            self.nc.psum_tensor(self.name + "_" + name, list(shape), dtype))

    def constv(self, val):
        if val not in self._consts:
            name = "cv%d" % len(self._consts)
            t = self.sb(name, [128, 1], F32)
            self.memset(t[:, :], float(val), writes=[name])
            self._consts[val] = (t, name)
        return self._consts[val]

    def op(self, eng, fn, reads=(), writes=(), dma=False, chan=None):
        ch = chan or eng
        waits = {}

        def _add(e2, c2):
            if e2 == ch == "tensor":
                return
            waits[e2] = max(waits.get(e2, 0), c2)

        for k in reads:
            w = self.writer.get(k)
            if w:
                _add(*w)
            if k.startswith("ps"):
                for e2, c2 in self.readers.get(k, {}).items():
                    _add(e2, c2)
        for k in writes:
            w = self.writer.get(k)
            if w:
                _add(*w)
            for e2, c2 in self.readers.get(k, {}).items():
                _add(e2, c2)
        if dma:
            self.dma_chans.add(ch)
            if chan is None and self.cnt.get(ch, 0):
                waits[ch] = max(waits.get(ch, 0), self.cnt[ch])
        for e2 in list(waits):
            if e2 in self.dma_chans:
                waits[e2] = max(waits[e2], self.cnt.get(e2, 0))
        inc = 16 if dma else 1
        self._sem(ch)
        self.cnt[ch] += inc
        cnt_after = self.cnt[ch]
        mysem = self.sems[ch]
        for e2 in waits:
            self._sem(e2)

        hw = self.hiwater.setdefault(eng, {})
        pruned = {}
        for e2, v in waits.items():
            if e2 == ch and v >= cnt_after:
                v = cnt_after - inc
            if hw.get(e2, -1) >= v:
                continue
            pruned[e2] = v
            hw[e2] = v

        def emit(engobj, fn=fn, waits=pruned, inc=inc):
            for e2, v in waits.items():
                engobj.wait_ge(self.sems[e2], v)
            ins = fn(engobj)
            ins.then_inc(mysem, inc)

        self.queues[eng].append(emit)
        for k in reads:
            self.readers.setdefault(k, {})[ch] = cnt_after
        for k in writes:
            self.writer[k] = (ch, cnt_after)
            self.readers[k] = {}
        return cnt_after

    def dma(self, q, out, in_, reads=(), writes=(), allow_slow=False,
            chan=None):
        self.op(q, lambda g: g.dma_start(
            out, in_, allow_slow_non_contiguous=allow_slow),
            reads=reads, writes=writes, dma=True, chan=chan)

    def dma_add(self, q, out, in_, reads=(), writes=()):
        self.op(q, lambda g: g.dma_start(out, in_, accum_op=ALU.add),
                reads=reads, writes=writes, dma=True)

    def mm(self, out, lhsT, rhs, start, stop, reads=(), writes=(), tp=None):
        self.op("tensor",
                lambda t: t.matmul(out, lhsT, rhs, start=start, stop=stop,
                                   tile_position=tp),
                reads=reads, writes=writes)

    def trans(self, out, in_, ident, reads=(), writes=()):
        self.op("tensor", lambda t: t.transpose(out, in_, ident),
                reads=reads, writes=writes)

    def act(self, out, in_, func, bias=0.0, scale=1.0, accum_out=None,
            reads=(), writes=()):
        if isinstance(bias, float) and bias not in (0.0, 1.0):
            t, nm = self.constv(bias)
            nparts = int(out.shape[0])
            bias = t[0:nparts, 0:1]
            reads = list(reads) + [nm]
        self.op("scalar",
                lambda s: s.activation(out, in_, func, bias=bias, scale=scale,
                                       accum_out=accum_out),
                reads=reads, writes=writes)

    def copy(self, eng, out, in_, reads=(), writes=()):
        if eng == "scalar":
            self.op(eng, lambda s: s.copy(out, in_), reads=reads, writes=writes)
        else:
            self.op(eng, lambda v: v.tensor_copy(out, in_), reads=reads,
                    writes=writes)

    def tt(self, out, a, b, alu, reads=(), writes=()):
        self.op("vector", lambda v: v.tensor_tensor(out, a, b, op=alu),
                reads=reads, writes=writes)

    def ts(self, out, a, s1, s2, op0, op1=ALU.bypass, reads=(), writes=()):
        self.op("vector",
                lambda v: v.tensor_scalar(out, a, s1, s2, op0=op0, op1=op1),
                reads=reads, writes=writes)

    def stt(self, out, in0, scalar, in1, op0, op1, reads=(), writes=()):
        self.op("vector",
                lambda v: v.scalar_tensor_tensor(out, in0, scalar, in1,
                                                 op0=op0, op1=op1),
                reads=reads, writes=writes)

    def red(self, out, in_, alu, negate=False, reads=(), writes=()):
        self.op("vector",
                lambda v: v.tensor_reduce(out, in_, axis=AX.X, op=alu,
                                          negate=negate),
                reads=reads, writes=writes)

    def recip(self, out, in_, reads=(), writes=()):
        self.op("vector", lambda v: v.reciprocal(out, in_), reads=reads,
                writes=writes)

    def memset(self, ap, val, reads=(), writes=()):
        self.op("vector", lambda v: v.memset(ap, val), reads=reads,
                writes=writes)

    def finish(self):
        for e in list(self.cnt):
            if self.cnt[e] == 0:
                continue
            self._sem(e)
        fin = {e: c for e, c in self.cnt.items() if c > 0}

        def final(g, fin=fin):
            for e, v in fin.items():
                g.wait_ge(self.sems[e], v)

        with self.nc.Block() as block:
            for e in ENGINES:
                q = self.queues[e]
                if not q and e != "gpsimd":
                    continue
                body = getattr(block, e)

                def run(engobj, q=q, last_e=(e == "gpsimd")):
                    for emit in q:
                        emit(engobj)
                    if last_e:
                        final(engobj)
                body(run)
        for e, c in self.cnt.items():
            assert c < 60000, f"sem overflow risk on {e}: {c}"
        assert len(self.sems) <= 24, f"too many sems: {len(self.sems)}"
        self.es.close()
        return self.nc


def _emit_fm_out(p, tm_tile, tm_key, eye, dst_fm, tile_i, ps_trs, stgs):
    stg, stg_key = stgs[tile_i % 2]
    for ct in range(8):
        ps_tr, psk = ps_trs[ct % 2]
        p.trans(ps_tr[:, 0:128], tm_tile[:, ct * 128:(ct + 1) * 128],
                eye[:, :], reads=[tm_key, "eye"], writes=[psk])
        p.copy("vector", stg[:, ct * 128:(ct + 1) * 128], ps_tr[:, 0:128],
               reads=[psk], writes=[stg_key])
    dst = bass.AP(dst_fm.tensor, tile_i * 128, [[T, 128], [128 * T, 8], [1, 128]])
    srcv = bass.AP(stg, 0, [[1024, 128], [128, 8], [1, 128]])
    p.dma("sync", dst, srcv, reads=[stg_key], chan="sync#t" + str(tile_i % 2))



def emit_norm(p, kind, x, gv, bv, eyef, xnF_o, xinF_o):
    bufs = []
    for par in range(2):
        s = str(par)
        bufs.append(dict(
            xt=(p.sb("xt" + s, [128, C], F32), "xt" + s),
            d=(p.sb("d" + s, [128, C], F32), "d" + s),
            sq=(p.sb("sq" + s, [128, C], F32), "sq" + s),
            xn_t=(p.sb("xn_t" + s, [128, C], F32), "xn_t" + s),
            xin_t=(p.sb("xin_t" + s, [128, C], F32), "xin_t" + s),
            u=(p.sb("u" + s, [128, C], F32), "u" + s),
            m=(p.sb("m" + s, [128, 1], F32), "m" + s),
            var=(p.sb("var" + s, [128, 1], F32), "var" + s),
            st=(p.sb("st" + s, [128, 1], F32), "st" + s),
            rstd=(p.sb("rstd" + s, [128, 1], F32), "rstd" + s)))
    grow = p.sb("grow", [1, C], F32)
    brow = p.sb("brow", [1, C], F32)
    ones1 = p.sb("ones1", [1, 128], F32)
    gb = p.sb("gb", [128, C], F32)
    bb = p.sb("bb", [128, C], F32)
    eye = p.sb("eye", [128, 128], F32)
    stg0 = p.sb("stg0", [128, 1024], F32)
    stg1 = p.sb("stg1", [128, 1024], F32)
    psb = p.ps("psb", [128, 512], F32)
    ps_tr = p.ps("ps_tr", [128, 512], F32)
    ps_tr2 = p.ps("ps_tr2", [128, 512], F32)
    ps_trs = [(ps_tr, "ps_tr"), (ps_tr2, "ps_tr2")]
    stgs = [(stg0, "stg0"), (stg1, "stg1")]

    p.dma("gpsimd", grow[:, :], gv, writes=["grow"])
    p.dma("gpsimd", brow[:, :], bv, writes=["brow"])
    p.dma("gpsimd", eye[:, :], eyef, writes=["eye"])
    p.memset(ones1[:, :], 1.0, writes=["ones1"])
    for j in range(C // 512):
        sl = slice(j * 512, (j + 1) * 512)
        p.mm(psb[:, :], ones1[:, :], grow[:, sl], True, True,
             reads=["ones1", "grow"], writes=["psb"])
        p.copy("vector", gb[:, sl], psb[:, :], reads=["psb"], writes=["gb"])
        p.mm(psb[:, :], ones1[:, :], brow[:, sl], True, True,
             reads=["ones1", "brow"], writes=["psb"])
        p.copy("vector", bb[:, sl], psb[:, :], reads=["psb"], writes=["bb"])

    inv_c = 1.0 / C
    for i in range(T // 128):
        B = bufs[i % 2]
        xt, xtk = B["xt"]; d, dk = B["d"]; sq, sqk = B["sq"]
        xn_t, xnk = B["xn_t"]; xin_t, xik = B["xin_t"]; u, uk = B["u"]
        m, mk = B["m"]; var, vk = B["var"]; st, stk = B["st"]
        rstd, rk = B["rstd"]
        rows = slice(i * 128, (i + 1) * 128)
        p.dma("gpsimd", xt[:, :], x[rows, :], writes=[xtk],
              chan="gpsimd#n" + str(i % 2))
        p.red(m[:, :], xt[:, :], ALU.add, reads=[xtk], writes=[mk])
        p.ts(m[:, :], m[:, :], inv_c, None, ALU.mult, reads=[mk], writes=[mk])
        p.ts(d[:, :], xt[:, :], m[:, 0:1], None, ALU.subtract,
             reads=[xtk, mk], writes=[dk])
        p.act(sq[:, :], d[:, :], AF.Square, reads=[dk], writes=[sqk])
        p.red(var[:, :], sq[:, :], ALU.add, reads=[sqk], writes=[vk])
        p.ts(var[:, :], var[:, :], inv_c, None, ALU.mult, reads=[vk],
             writes=[vk])
        p.act(st[:, :], var[:, :], AF.Sqrt, bias=1e-5, reads=[vk],
              writes=[stk])
        p.recip(rstd[:, :], st[:, :], reads=[stk], writes=[rk])
        p.ts(xn_t[:, :], d[:, :], rstd[:, 0:1], None, ALU.mult,
             reads=[dk, rk], writes=[xnk])
        _emit_fm_out(p, xn_t, xnk, eye, xnF_o, i, ps_trs, stgs)
        if kind == "ln2":
            p.red(m[:, :], xn_t[:, :], ALU.add, reads=[xnk], writes=[mk])
            p.ts(m[:, :], m[:, :], inv_c, None, ALU.mult, reads=[mk],
                 writes=[mk])
            p.ts(d[:, :], xn_t[:, :], m[:, 0:1], None, ALU.subtract,
                 reads=[xnk, mk], writes=[dk])
            p.act(sq[:, :], d[:, :], AF.Square, reads=[dk], writes=[sqk])
            p.red(var[:, :], sq[:, :], ALU.add, reads=[sqk], writes=[vk])
            p.ts(var[:, :], var[:, :], inv_c, None, ALU.mult, reads=[vk],
                 writes=[vk])
            p.act(st[:, :], var[:, :], AF.Sqrt, bias=1e-5, reads=[vk],
                  writes=[stk])
            p.recip(rstd[:, :], st[:, :], reads=[stk], writes=[rk])
            p.ts(u[:, :], d[:, :], rstd[:, 0:1], None, ALU.mult,
                 reads=[dk, rk], writes=[uk])
            p.tt(u[:, :], u[:, :], gb[:, :], ALU.mult, reads=[uk, "gb"],
                 writes=[uk])
            p.tt(xin_t[:, :], u[:, :], bb[:, :], ALU.add, reads=[uk, "bb"],
                 writes=[xik])
        else:
            p.act(sq[:, :], xn_t[:, :], AF.Square, reads=[xnk],
                  writes=[sqk])
            p.red(var[:, :], sq[:, :], ALU.add, reads=[sqk], writes=[vk])
            p.ts(var[:, :], var[:, :], inv_c, None, ALU.mult, reads=[vk],
                 writes=[vk])
            p.act(st[:, :], var[:, :], AF.Sqrt, bias=1e-5, reads=[vk],
                  writes=[stk])
            p.recip(rstd[:, :], st[:, :], reads=[stk], writes=[rk])
            p.ts(u[:, :], xn_t[:, :], rstd[:, 0:1], None, ALU.mult,
                 reads=[xnk, rk], writes=[uk])
            p.tt(xin_t[:, :], u[:, :], gb[:, :], ALU.mult, reads=[uk, "gb"],
                 writes=[xik])
        _emit_fm_out(p, xin_t, xik, eye, xinF_o, i, ps_trs, stgs)


def emit_router(p, inF, W, b, eyef, mgT_o=None, probsT_o=None):
    xf = p.sb("xf", [128, 8 * T], F32)
    wsb = p.sb("wsb", [128, 64], F32)
    brow = p.sb("brow", [1, 8], F32)
    ones1 = p.sb("ones1", [1, 128], F32)
    pe = p.sb("pe", [128, 8], F32)
    sums = p.sb("sums", [128, 1], F32)
    rs = p.sb("rs", [128, 1], F32)
    pr = p.sb("pr", [128, 8], F32)
    gate = p.sb("gate", [128, 1], F32)
    msk = p.sb("msk", [128, 8], F32)
    mgt = p.sb("mgt", [128, 8], F32)
    nm = p.sb("nm", [128, 1], F32)
    eye = p.sb("eye", [128, 128], F32)
    mgT_sb = p.sb("mgT_sb", [8, T], F32)
    prT_sb = p.sb("prT_sb", [8, T], F32)
    pl = p.ps("ps_l", [128, 8], F32)
    ps_tr = p.ps("ps_tr", [128, 512], F32)

    for kc in range(8):
        p.dma("gpsimd", xf[:, kc * T:(kc + 1) * T],
              inF[kc * 128:(kc + 1) * 128, :], writes=["xf"])
        p.dma("gpsimd", wsb[:, kc * 8:(kc + 1) * 8],
              W[kc * 128:(kc + 1) * 128, :], writes=["wsb"])
    p.dma("gpsimd", brow[:, :], b, writes=["brow"])
    p.dma("gpsimd", eye[:, :], eyef, writes=["eye"])
    p.memset(ones1[:, :], 1.0, writes=["ones1"])

    for i in range(T // 128):
        for kc in range(8):
            p.mm(pl[:, :], xf[:, kc * T + i * 128: kc * T + (i + 1) * 128],
                 wsb[:, kc * 8:(kc + 1) * 8], kc == 0, False,
                 reads=["xf", "wsb"], writes=["ps_l"])
        p.mm(pl[:, :], ones1[:, :], brow[:, :], False, True,
             reads=["ones1", "brow"], writes=["ps_l"])
        p.red(nm[:, :], pl[:, :], ALU.max, negate=True, reads=["ps_l"],
              writes=["nm"])
        p.act(pe[:, :], pl[:, :], AF.Exp, bias=nm[:, 0:1],
              accum_out=sums[:, 0:1], reads=["ps_l", "nm"],
              writes=["pe", "sums"])
        p.recip(rs[:, :], sums[:, :], reads=["sums"], writes=["rs"])
        p.ts(pr[:, :], pe[:, :], rs[:, 0:1], None, ALU.mult,
             reads=["pe", "rs"], writes=["pr"])
        if probsT_o is not None:
            p.trans(ps_tr[0:8, 0:128], pr[:, :], eye[:, :],
                    reads=["pr", "eye"], writes=["ps_tr"])
            p.copy("vector", prT_sb[:, i * 128:(i + 1) * 128],
                   ps_tr[0:8, 0:128], reads=["ps_tr"], writes=["prT_sb"])
        if mgT_o is not None:
            p.red(gate[:, :], pr[:, :], ALU.max, reads=["pr"], writes=["gate"])
            p.ts(msk[:, :], pr[:, :], gate[:, 0:1], None, ALU.is_ge,
                 reads=["pr", "gate"], writes=["msk"])
            p.ts(mgt[:, :], msk[:, :], gate[:, 0:1], None, ALU.mult,
                 reads=["msk", "gate"], writes=["mgt"])
            p.trans(ps_tr[0:8, 0:128], mgt[:, :], eye[:, :],
                    reads=["mgt", "eye"], writes=["ps_tr"])
            p.copy("vector", mgT_sb[:, i * 128:(i + 1) * 128],
                   ps_tr[0:8, 0:128], reads=["ps_tr"], writes=["mgT_sb"])
    if mgT_o is not None:
        p.dma("sync", mgT_o, mgT_sb[:, :], reads=["mgT_sb"])
    if probsT_o is not None:
        p.dma("sync", probsT_o, prT_sb[:, :], reads=["prT_sb"])


def emit_attn0(p, xnF, wqT, wkT, wvT, woT, s_col, cosI, sinI, P128, onesH,
               onesHT, eyef, sF_o):
    xn_bf = p.sb("xn_bf", [128, 8 * T], BF16)
    w_bf = p.sb("w_bf", [128, 8 * C], BF16)
    q_bf = p.sb("q_bf", [128, 8 * T], BF16)
    k_bf = p.sb("k_bf", [128, 8 * T], BF16)
    v_bf = p.sb("v_bf", [128, 16 * C], BF16)
    cosb = p.sb("cosb", [128, T], F32)
    sinb = p.sb("sinb", [128, T], F32)
    exp_bf = p.sb("exp_bf", [128, 16 * 512], BF16)
    nmaxb = p.sb("nmaxb", [128, 512], F32)
    tsum0 = p.sb("tsum0", [128, 512], F32)
    tsum1 = p.sb("tsum1", [128, 512], F32)
    p128 = p.sb("p128", [128, 128], F32)
    oh = p.sb("oh", [128, 2], F32)
    oht = p.sb("oht", [2, 128], F32)
    eye = p.sb("eye", [128, 128], F32)
    scol = p.sb("scol", [128, 8], F32)
    ksc = p.sb("ksc", [128, 8], F32)
    ones1 = p.sb("ones1", [1, 128], F32)
    onesk = p.sb("onesk", [128, 1], BF16)
    qraw = p.sb("qraw", [128, 512], F32)
    qraw2 = p.sb("qraw2", [128, 512], F32)
    qr = p.sb("qr", [128, 512], F32)
    qr2 = p.sb("qr2", [128, 512], F32)
    sqv = p.sb("sqv", [128, 512], F32)
    sqv2 = p.sb("sqv2", [128, 512], F32)
    rnv = p.sb("rnv", [2, 512], F32)
    maxJP = p.sb("maxJP", [128, 512], F32)
    nmax = p.sb("nmax", [128, 1], F32)
    nrow = p.sb("nrow", [1, 512], F32)
    z_top = p.sb("z_top", [128, 512], BF16)
    z_bot = p.sb("z_bot", [128, 512], BF16)
    sstage = p.sb("sstage", [128, 512], F32)

    ps0 = p.ps("ps0", [128, 512], F32)
    ps1 = p.ps("ps1", [128, 512], F32)
    ps2 = p.ps("ps2", [128, 512], F32)
    ps3 = p.ps("ps3", [128, 512], F32)
    ps4 = p.ps("ps4", [128, 512], F32)
    ps5 = p.ps("ps5", [128, 512], F32)
    ps6 = p.ps("ps6", [128, 512], F32)

    p.dma("gpsimd", cosb[:, :], cosI, writes=["cosb"])
    p.dma("gpsimd", sinb[:, :], sinI, writes=["sinb"])
    p.dma("gpsimd", p128[:, :], P128, writes=["p128"])
    p.dma("gpsimd", oh[:, :], onesH, writes=["oh"])
    p.dma("gpsimd", oht[:, :], onesHT, writes=["oht"])
    p.dma("gpsimd", eye[:, :], eyef, writes=["eye"])
    p.dma("gpsimd", scol[:, :], s_col, writes=["scol"])
    p.memset(ones1[:, :], 1.0, writes=["ones1"])
    p.op("vector", lambda v: v.memset(onesk[:, :], 1.0), writes=["onesk"])
    p.act(ksc[:, :], scol[:, :], AF.Square, scale=math.sqrt(8192.0),
          reads=["scol"], writes=["ksc"])
    for kc in range(8):
        p.dma("gpsimd", xn_bf[:, kc * T:(kc + 1) * T],
              xnF[kc * 128:(kc + 1) * 128, :], writes=["xn_bf"],
              chan="gpsimd#x" + str(kc % 2))

    qraws = [(qraw, "qraw"), (qraw2, "qraw2")]
    qrs = [(qr, "qr"), (qr2, "qr2")]
    sqvs = [(sqv, "sqv"), (sqv2, "sqv2")]
    chain_ps = [(ps0, "ps0"), (ps5, "ps5")]
    for which, wT, dest in (("q", wqT, q_bf), ("k", wkT, k_bf)):
        destkey = "q_bf" if which == "q" else "k_bf"
        for kc in range(8):
            p.dma("gpsimd", w_bf[:, kc * C:(kc + 1) * C],
                  wT[kc * 128:(kc + 1) * 128, :], writes=["w_bf" + str(kc)],
                  chan="gpsimd#x" + str(kc % 2))
        its = [(ct, tc) for ct in range(8) for tc in range(4)]

        def block(i, its=its):
            ct, tc = its[i]
            psC, psk = chain_ps[i % 2]
            qw, qwk = qraws[i % 2]
            for kc in range(8):
                p.mm(psC[:, :],
                     w_bf[:, kc * C + ct * 128: kc * C + (ct + 1) * 128],
                     xn_bf[:, kc * T + tc * 512: kc * T + (tc + 1) * 512],
                     kc == 0, kc == 7,
                     reads=["w_bf" + str(kc), "w_bf6", "w_bf7", "xn_bf"],
                     writes=[psk])
            p.copy("scalar", qw[:, :], psC[:, :], reads=[psk], writes=[qwk])

        def rotgrp(i, its=its):
            ct, tc = its[i]
            tsl = slice(tc * 512, (tc + 1) * 512)
            qw, qwk = qraws[i % 2]
            qo, qok = qrs[i % 2]
            sv, svk = sqvs[i % 2]
            p.mm(ps1[:, :], p128[:, :], qw[:, :], True, True,
                 reads=["p128", qwk], writes=["ps1"])
            p.tt(tsum1[:, :], ps1[:, :], sinb[:, tsl], ALU.mult,
                 reads=["ps1", "sinb"], writes=["tsum1"])
            p.tt(tsum0[:, :], qw[:, :], cosb[:, tsl], ALU.mult,
                 reads=[qwk, "cosb"], writes=["tsum0"])
            p.tt(qo[:, :], tsum0[:, :], tsum1[:, :], ALU.add,
                 reads=["tsum0", "tsum1"], writes=[qok])
            p.act(sv[:, :], qo[:, :], AF.Square, reads=[qok], writes=[svk])

        def helpersB(i, which=which, dest=dest, destkey=destkey, its=its):
            ct, tc = its[i]
            qo, qok = qrs[i % 2]
            sv, svk = sqvs[i % 2]
            p.mm(ps2[0:2, :], oh[:, :], sv[:, :], True, True,
                 reads=["oh", svk], writes=["ps2"])
            p.act(rnv[:, :], ps2[0:2, :], AF.Sqrt, bias=1e-24,
                  reads=["ps2"], writes=["rnv"])
            p.recip(nmaxb[0:2, :], rnv[:, :], reads=["rnv"],
                    writes=["nmaxb"])
            p.mm(ps3[:, :], oht[:, :], nmaxb[0:2, :], True, True,
                 reads=["oht", "nmaxb"], writes=["ps3"])
            dsl = slice(ct * T + tc * 512, ct * T + (tc + 1) * 512)
            if which == "q":
                p.tt(dest[:, dsl], qo[:, :], ps3[:, :], ALU.mult,
                     reads=[qok, "ps3"], writes=[destkey])
            else:
                p.stt(dest[:, dsl], qo[:, :], ksc[:, ct:ct + 1], ps3[:, :],
                      ALU.mult, ALU.mult, reads=[qok, "ksc", "ps3"],
                      writes=[destkey])

        block(0)
        block(1)
        rotgrp(0)
        for i in range(2, 32):
            block(i)
            rotgrp(i - 1)
            helpersB(i - 2)
        rotgrp(31)
        helpersB(30)
        helpersB(31)

    for kc in range(8):
        p.dma("gpsimd", w_bf[:, kc * C:(kc + 1) * C],
              wvT[kc * 128:(kc + 1) * 128, :], writes=["w_bf" + str(kc)],
              chan="gpsimd#x" + str(kc % 2))
    for vi, (tt, ic) in enumerate([(tt, ic) for tt in range(16)
                                   for ic in range(2)]):
        psC, psk = chain_ps[vi % 2]
        for kc in range(8):
            p.mm(psC[:, :],
                 xn_bf[:, kc * T + tt * 128: kc * T + (tt + 1) * 128],
                 w_bf[:, kc * C + ic * 512: kc * C + (ic + 1) * 512],
                 kc == 0, kc == 7,
                 reads=["xn_bf", "w_bf" + str(kc), "w_bf6", "w_bf7"],
                 writes=[psk])
        p.copy("scalar", v_bf[:, tt * C + ic * 512: tt * C + (ic + 1) * 512],
               psC[:, :], reads=[psk], writes=["v_bf"])

    P2R = [(ps0, "ps0"), (ps1, "ps1"), (ps5, "ps5"), (ps6, "ps6")]
    p.op("vector", lambda v: v.memset(z_top[:, :], 0.0), writes=["zt"])
    p.op("vector", lambda v: v.memset(z_bot[:, :], 0.0), writes=["zb"])
    for cth in range(8):
        for qc in range(4):
            qsl = slice(cth * T + qc * 512, cth * T + (qc + 1) * 512)
            p.op("vector", lambda v, qsl=qsl: v.tensor_copy(
                z_top[0:64, :], q_bf[0:64, qsl]),
                reads=["q_bf"], writes=["zt"])
            p.op("vector", lambda v, qsl=qsl: v.tensor_copy(
                z_bot[64:128, :], q_bf[64:128, qsl]),
                reads=["q_bf"], writes=["zb"])
            for sub in range(2):
                h = 2 * cth + sub
                bp = sub * 64
                zq, zqk = (z_top, "zt") if sub == 0 else (z_bot, "zb")
                for kt in range(16):
                    psA, key = P2R[kt % 4]
                    p.mm(psA[:, :],
                         k_bf[:, cth * T + kt * 128: cth * T + (kt + 1) * 128],
                         zq[:, :], True, True,
                         reads=["k_bf", zqk], writes=[key])
                    if kt == 0:
                        p.copy("vector", maxJP[:, :], psA[:, :], reads=[key],
                               writes=["maxJP"])
                    else:
                        p.tt(maxJP[:, :], maxJP[:, :], psA[:, :], ALU.max,
                             reads=["maxJP", key], writes=["maxJP"])
                for j in range(4):
                    p.trans(ps2[:, 0:128], maxJP[:, j * 128:(j + 1) * 128],
                            eye[:, :], reads=["maxJP", "eye"], writes=["ps2"])
                    p.red(nmax[:, :], ps2[:, 0:128], ALU.max, negate=True,
                          reads=["ps2"], writes=["nmax"])
                    p.trans(ps3[0:1, 0:128], nmax[:, 0:1], eye[:, :],
                            reads=["nmax", "eye"], writes=["ps3"])
                    p.copy("vector", nrow[:, j * 128:(j + 1) * 128],
                           ps3[0:1, 0:128], reads=["ps3"], writes=["nrow"])
                p.mm(ps3[:, :], ones1[:, :], nrow[:, :], True, True,
                     reads=["ones1", "nrow"], writes=["ps3"])
                p.copy("vector", nmaxb[:, :], ps3[:, :], reads=["ps3"],
                       writes=["nmaxb"])
                for kt in range(16):
                    psA, key = P2R[kt % 4]
                    p.mm(psA[:, :],
                         k_bf[:, cth * T + kt * 128: cth * T + (kt + 1) * 128],
                         zq[:, :], True, True,
                         reads=["k_bf", zqk], writes=[key])
                    ts_t, ts_k = (tsum0, "tsum0") if kt % 2 == 0 else \
                        (tsum1, "tsum1")
                    p.tt(ts_t[:, :], psA[:, :], nmaxb[:, :], ALU.add,
                         reads=[key, "nmaxb"], writes=[ts_k])
                    p.act(exp_bf[:, kt * 512:(kt + 1) * 512], ts_t[:, :],
                          AF.Exp, reads=[ts_k], writes=["exp_bf%d" % kt])
                for kt in range(16):
                    p.mm(ps2[0:1, :], onesk[:, :],
                         exp_bf[:, kt * 512:(kt + 1) * 512], kt == 0, kt == 15,
                         reads=["onesk", "exp_bf%d" % kt], writes=["ps2"])
                    p.mm(ps4[0:64, :],
                         v_bf[:, kt * C + h * 64: kt * C + (h + 1) * 64],
                         exp_bf[:, kt * 512:(kt + 1) * 512], kt == 0, kt == 15,
                         reads=["v_bf", "exp_bf%d" % kt], writes=["ps4"])
                p.recip(nrow[:, :], ps2[0:1, :], reads=["ps2"],
                        writes=["nrow"])
                p.mm(ps3[:, :], ones1[:, :], nrow[:, :], True, True,
                     reads=["ones1", "nrow"], writes=["ps3"])
                p.copy("vector", qr[:, :], ps3[:, :], reads=["ps3"],
                       writes=["qr"])
                p.tt(xn_bf[bp:bp + 64, qsl], ps4[0:64, :], qr[0:64, :],
                     ALU.mult, reads=["ps4", "qr"], writes=["xn_bf"])

    for kc in range(8):
        p.dma("gpsimd", w_bf[:, kc * C:(kc + 1) * C],
              woT[kc * 128:(kc + 1) * 128, :], writes=["w_bf" + str(kc)],
              chan="gpsimd#x" + str(kc % 2))
    ostg = [(sstage, "sstage"), (qraw, "qraw")]
    for oi, (ct, tc) in enumerate([(ct, tc) for ct in range(8)
                                   for tc in range(4)]):
        psC, psk = chain_ps[oi % 2]
        stg, sgk = ostg[oi % 2]
        for kc in range(8):
            p.mm(psC[:, :],
                 w_bf[:, kc * C + ct * 128: kc * C + (ct + 1) * 128],
                 xn_bf[:, kc * T + tc * 512: kc * T + (tc + 1) * 512],
                 kc == 0, kc == 7,
                 reads=["w_bf" + str(kc), "w_bf6", "w_bf7", "xn_bf"],
                 writes=[psk])
        p.copy("scalar", stg[:, :], psC[:, :], reads=[psk], writes=[sgk])
        p.dma("sync", sF_o[ct * 128:(ct + 1) * 128,
                           tc * 512:(tc + 1) * 512],
              stg[:, :], reads=[sgk], chan="sync#o" + str(oi % 2))


def emit_moe(p, l1_mode, xinF, W1, B1, W2, B2, gF_o, mgT_src=None,
             agout=None, probsT_mine=None, blk16=None, blk16T=None,
             mg_scratch=None):
    xin_bf = p.sb("xin_bf", [128, 8 * T], BF16)
    y_acc = p.sb("y_acc", [128, 8 * T], F32)
    w1a = p.sb("w1a", [128, 8 * 512], BF16)
    w1b = p.sb("w1b", [128, 8 * 512], BF16)
    w2a = p.sb("w2a", [128, 4 * C], BF16)
    w2b = p.sb("w2b", [128, 4 * C], BF16)
    b1s = p.sb("b1s", [128, 64], F32)
    b2s = p.sb("b2s", [8, C], F32)
    mgs = p.sb("mgs", [8, T], F32)
    mgrow = p.sb("mgrow", [1, 512], F32)
    mgball = p.sb("mgball", [128, 4 * 512], F32)
    hrelu = p.sb("hrelu", [128, 512], F32)
    hrelu2 = p.sb("hrelu2", [128, 512], F32)
    h_bf = p.sb("h_bf", [128, 4 * 512], BF16)
    h_bf2 = p.sb("h_bf2", [128, 4 * 512], BF16)
    ones1 = p.sb("ones1", [1, 128], F32)
    ps_h0 = p.ps("ps_h0", [128, 512], F32)
    ps_h1 = p.ps("ps_h1", [128, 512], F32)
    ps_y0 = p.ps("ps_y0", [128, 512], F32)
    ps_y1 = p.ps("ps_y1", [128, 512], F32)
    ps_m = p.ps("ps_m", [128, 512], F32)

    p.memset(ones1[:, :], 1.0, writes=["ones1"])
    for kc in range(8):
        p.dma("gpsimd", xin_bf[:, kc * T:(kc + 1) * T],
              xinF[kc * 128:(kc + 1) * 128, :], writes=["xin_bf"])
    p.dma("gpsimd", b2s[:, :], B2, writes=["b2s"])

    if l1_mode:
        pAll = p.sb("pAlls", [128, 1024], F32)
        msel = p.sb("msel", [8, T], F32)
        agout_rows = agout.rearrange("p (c t) -> (p c) t", c=2)
        lo = p.sb("lo", [128, 1], F32)
        hi = p.sb("hi", [128, 1], F32)
        mid = p.sb("mid", [128, 1], F32)
        msk = p.sb("msk", [128, 1024], F32)
        rowc = p.sb("rowc", [128, 1], F32)
        pred8 = p.sb("pred8", [8, 1], F32)
        predb = p.sb("predb", [128, 1], F32)
        d1 = p.sb("d1", [128, 1], F32)
        d2 = p.sb("d2", [128, 1], F32)
        th8 = p.sb("th8", [8, 1], F32)
        b16 = p.sb("b16", [128, 8], F32)
        b16t = p.sb("b16t", [8, 128], F32)
        gsum = p.sb("gsum", [1, T], F32)
        rgs = p.sb("rgs", [1, T], F32)
        rgsb = p.sb("rgsb", [128, 512], F32)
        ps_c = p.ps("ps_c", [8, 1], F32)
        ps_p = p.ps("ps_p", [128, 1], F32)
        ps_g = p.ps("ps_g", [1, 512], F32)

        p.dma("gpsimd", pAll[:, :], agout_rows, writes=["pAlls"])
        p.dma("gpsimd", mgs[:, :], probsT_mine, writes=["mgs"])
        p.dma("gpsimd", b16[:, :], blk16, writes=["b16"])
        p.dma("gpsimd", b16t[:, :], blk16T, writes=["b16t"])
        p.memset(lo[:, :], 0.0, writes=["lo"])
        p.memset(hi[:, :], 1.0, writes=["hi"])
        for _ in range(40):
            p.tt(mid[:, :], lo[:, :], hi[:, :], ALU.add, reads=["lo", "hi"],
                 writes=["mid"])
            p.ts(mid[:, :], mid[:, :], 0.5, None, ALU.mult, reads=["mid"],
                 writes=["mid"])
            p.ts(msk[:, :], pAll[:, :], mid[:, 0:1], None, ALU.is_ge,
                 reads=["pAlls", "mid"], writes=["msk"])
            p.red(rowc[:, :], msk[:, :], ALU.add, reads=["msk"],
                  writes=["rowc"])
            p.mm(ps_c[:, :], b16[:, :], rowc[:, :], True, True,
                 reads=["b16", "rowc"], writes=["ps_c"])
            p.ts(pred8[:, :], ps_c[:, :], 2048.0, None, ALU.is_ge,
                 reads=["ps_c"], writes=["pred8"])
            p.mm(ps_p[:, :], b16t[:, :], pred8[:, :], True, True,
                 reads=["b16t", "pred8"], writes=["ps_p"])
            p.copy("vector", predb[:, :], ps_p[:, :], reads=["ps_p"],
                   writes=["predb"])
            p.tt(d1[:, :], mid[:, :], lo[:, :], ALU.subtract,
                 reads=["mid", "lo"], writes=["d1"])
            p.tt(d2[:, :], hi[:, :], mid[:, :], ALU.subtract,
                 reads=["hi", "mid"], writes=["d2"])
            p.stt(lo[:, :], predb[:, :], d1[:, 0:1], lo[:, :], ALU.mult,
                  ALU.add, reads=["predb", "d1", "lo"], writes=["lo"])
            p.stt(hi[:, :], predb[:, :], d2[:, 0:1], mid[:, :], ALU.mult,
                  ALU.add, reads=["predb", "d2", "mid"], writes=["hi"])
        p.ts(b16[:, :], b16[:, :], 1.0 / 16.0, None, ALU.mult,
             reads=["b16"], writes=["b16"])
        p.mm(ps_c[:, :], b16[:, :], lo[:, :], True, True,
             reads=["b16", "lo"], writes=["ps_c"])
        p.copy("vector", th8[:, :], ps_c[:, :], reads=["ps_c"], writes=["th8"])
        p.ts(msel[:, :], mgs[:, :], th8[:, 0:1], None, ALU.is_ge,
             reads=["mgs", "th8"], writes=["msel"])
        p.tt(mgs[:, :], msel[:, :], mgs[:, :], ALU.mult,
             reads=["msel", "mgs"], writes=["mgs"])
        p.op("vector", lambda v: v.memset(d1[:, :], 1.0), writes=["d1"])
        for j in range(4):
            jsl = slice(j * 512, (j + 1) * 512)
            p.mm(ps_g[:, :], d1[0:8, 0:1], mgs[:, jsl], True, True,
                 reads=["d1", "mgs"], writes=["ps_g"])
            p.copy("vector", gsum[:, jsl], ps_g[:, :], reads=["ps_g"],
                   writes=["gsum"])
        p.ts(gsum[:, :], gsum[:, :], 1e-9, None, ALU.max, reads=["gsum"],
             writes=["gsum"])
        p.recip(rgs[:, :], gsum[:, :], reads=["gsum"], writes=["rgs"])
        p.dma("sync", mg_scratch, mgs[:, :], reads=["mgs"],
              writes=["mgs_dram"])
        mg_rows = mg_scratch
    else:
        p.dma("gpsimd", mgs[:, :], mgT_src, writes=["mgs"])
        mg_rows = mgT_src

    p.memset(y_acc[:, :], 0.0, writes=["y_acc"])
    actf = AF.Gelu if l1_mode else AF.Relu

    w1bufs = [(w1a, "w1a", "gpsimd#wa"), (w1b, "w1b", "gpsimd#wb")]
    w2bufs = [(w2a, "w2a", "gpsimd#w2a"), (w2b, "w2b", "gpsimd#w2b")]
    chunks = [(e, fc) for e in range(8) for fc in range(8)]

    def load_chunk(i):
        e, fc = chunks[i]
        w1t, w1k, ch1 = w1bufs[i % 2]
        w2t, w2k, ch2 = w2bufs[i % 2]
        for kc in range(8):
            p.dma("gpsimd", w1t[:, kc * 512:(kc + 1) * 512],
                  W1[e, kc * 128:(kc + 1) * 128, fc * 512:(fc + 1) * 512],
                  writes=[w1k + str(kc)], chan=ch1)
        for fs in range(4):
            p.dma("gpsimd", w2t[:, fs * C:(fs + 1) * C],
                  W2[e, fc * 512 + fs * 128: fc * 512 + (fs + 1) * 128, :],
                  writes=[w2k + str(fs)], chan=ch2)

    load_chunk(0)
    h_bufs = [(h_bf, "h_bf"), (h_bf2, "h_bf2")]
    hre_bufs = [(hrelu, "hrelu"), (hrelu2, "hrelu2")]

    def emit_fc1(i, tc, jpar):
        e, fc = chunks[i]
        w1t, w1k, _ = w1bufs[i % 2]
        ht, hk = h_bufs[jpar]
        b1k = "b1s" + str(e % 2)
        b1off = (e % 2) * 32
        tslk = slice(tc * 512, (tc + 1) * 512)
        for fs in range(4):
            psh, pshk = (ps_h0, "ps_h0") if fs % 2 == 0 else (ps_h1, "ps_h1")
            for kc in range(8):
                p.mm(psh[:, :],
                     w1t[:, kc * 512 + fs * 128: kc * 512 + (fs + 1) * 128],
                     xin_bf[:, kc * T + tc * 512: kc * T + (tc + 1) * 512],
                     kc == 0, kc == 7,
                     reads=[w1k + str(kc), w1k + "7", "xin_bf"],
                     writes=[pshk])
            hr, hrk = hre_bufs[fs % 2]
            p.act(hr[:, :], psh[:, :], actf,
                  bias=b1s[:, b1off + fc * 4 + fs: b1off + fc * 4 + fs + 1],
                  reads=[pshk, b1k], writes=[hrk])
            p.tt(ht[:, fs * 512:(fs + 1) * 512], hr[:, :],
                 mgball[:, tslk], ALU.mult, reads=[hrk, "mgball"],
                 writes=[hk])

    def emit_fc2(i, tc, jpar):
        w2t, w2k, _ = w2bufs[i % 2]
        ht, hk = h_bufs[jpar]
        for dt in range(8):
            psy, psyk = (ps_y0, "ps_y0") if dt % 2 == 0 else (ps_y1, "ps_y1")
            for fs in range(4):
                p.mm(psy[:, :],
                     w2t[:, fs * C + dt * 128: fs * C + (dt + 1) * 128],
                     ht[:, fs * 512:(fs + 1) * 512],
                     fs == 0, fs == 3,
                     reads=[w2k + str(fs), w2k + "3", hk],
                     writes=[psyk])
            ysl = slice(dt * T + tc * 512, dt * T + (tc + 1) * 512)
            p.tt(y_acc[:, ysl], y_acc[:, ysl], psy[:, :], ALU.add,
                 reads=["y_acc", psyk], writes=["y_acc"])

    prev = None
    for j, (i, tc) in enumerate([(i, tc) for i in range(len(chunks))
                                 for tc in range(4)]):
        e, fc = chunks[i]
        if tc == 1 and i + 1 < len(chunks):
            load_chunk(i + 1)
        if tc == 0:
            if fc == 0:
                p.dma("gpsimd", b1s[:, (e % 2) * 32:(e % 2) * 32 + 32],
                      B1[e, :, :], writes=["b1s" + str(e % 2)])
                for tcb in range(4):
                    p.dma("gpsimd", mgrow[:, :],
                          mg_rows[e:e + 1, tcb * 512:(tcb + 1) * 512],
                          reads=["mgs_dram"], writes=["mgrow"])
                    p.mm(ps_m[:, :], ones1[:, :], mgrow[:, :], True, True,
                         reads=["ones1", "mgrow"], writes=["ps_m"])
                    p.copy("vector", mgball[:, tcb * 512:(tcb + 1) * 512],
                           ps_m[:, :], reads=["ps_m"], writes=["mgball"])
        emit_fc1(i, tc, j % 2)
        if prev is not None:
            emit_fc2(*prev)
        prev = (i, tc, j % 2)
    emit_fc2(*prev)

    for tc in range(4):
        tsl = slice(tc * 512, (tc + 1) * 512)
        if l1_mode:
            p.mm(ps_m[:, :], ones1[:, :], rgs[0:1, tsl], True, True,
                 reads=["ones1", "rgs"], writes=["ps_m"])
            p.copy("vector", rgsb[:, :], ps_m[:, :], reads=["ps_m"],
                   writes=["rgsb"])
        for dt in range(8):
            ysl = slice(dt * T + tc * 512, dt * T + (tc + 1) * 512)
            p.mm(ps_y0[:, :], b2s[:, dt * 128:(dt + 1) * 128], mgs[:, tsl],
                 True, True, reads=["b2s", "mgs"], writes=["ps_y0"])
            p.tt(y_acc[:, ysl], y_acc[:, ysl], ps_y0[:, :], ALU.add,
                 reads=["y_acc", "ps_y0"], writes=["y_acc"])
            if l1_mode:
                p.tt(y_acc[:, ysl], y_acc[:, ysl], rgsb[:, :], ALU.mult,
                     reads=["y_acc", "rgsb"], writes=["y_acc"])
    for dt in range(8):
        p.dma("sync", gF_o[dt * 128:(dt + 1) * 128, :],
              y_acc[:, dt * T:(dt + 1) * T], reads=["y_acc"])


def emit_combine(p, xF, sF, gF, eyef, oF=None, oTM=None):
    xt = p.sb("xt", [128, T], F32)
    st_ = p.sb("st_", [128, T], F32)
    gt = p.sb("gt", [128, T], F32)
    sg = p.sb("sg", [128, T], F32)
    ot = p.sb("ot", [128, T], F32)
    eye = p.sb("eye", [128, 128], F32)
    stg0 = p.sb("stg0", [128, 2048], F32)
    stg1 = p.sb("stg1", [128, 2048], F32)
    ps_tr = p.ps("ps_tr", [128, 512], F32)
    ps_tr2 = p.ps("ps_tr2", [128, 512], F32)
    p.dma("gpsimd", eye[:, :], eyef, writes=["eye"])
    for ct in range(8):
        rows = slice(ct * 128, (ct + 1) * 128)
        ch = "gpsimd#c" + str(ct % 2)
        p.dma("gpsimd", xt[:, :], xF[rows, :], writes=["xt"], chan=ch)
        p.dma("gpsimd", st_[:, :], sF[rows, :], writes=["st_"], chan=ch)
        p.dma("gpsimd", gt[:, :], gF[rows, :], writes=["gt"], chan=ch)
        p.act(sg[:, :], gt[:, :], AF.Silu, reads=["gt"], writes=["sg"])
        p.tt(sg[:, :], st_[:, :], sg[:, :], ALU.mult, reads=["st_", "sg"],
             writes=["sg"])
        p.tt(ot[:, :], xt[:, :], sg[:, :], ALU.add, reads=["xt", "sg"],
             writes=["ot"])
        if oF is not None:
            p.dma("sync", oF[rows, :], ot[:, :], reads=["ot"])
        if oTM is not None:
            stg, sk = (stg0, "stg0") if ct % 2 == 0 else (stg1, "stg1")
            for tt in range(16):
                pst, psk = (ps_tr, "ps_tr") if tt % 2 == 0 else (ps_tr2, "ps_tr2")
                p.trans(pst[:, 0:128], ot[:, tt * 128:(tt + 1) * 128],
                        eye[:, :], reads=["ot", "eye"], writes=[psk])
                p.copy("vector", stg[:, tt * 128:(tt + 1) * 128],
                       pst[:, 0:128], reads=[psk], writes=[sk])
            dst = bass.AP(oTM.tensor, ct * 128, [[C, 128], [128 * C, 16], [1, 128]])
            srcv = bass.AP(stg, 0, [[2048, 128], [128, 16], [1, 128]])
            p.dma("sync", dst, srcv, reads=[sk], chan="sync#t" + str(ct % 2))


def emit_attn1(p, xnF, wqT, wkT, wvT, wgT, woT, gnw, cosH, sinH, lamq_d,
               lamk_d, mdel, s1F_o):
    xn_bf = p.sb("xn_bf", [128, 8 * T], BF16)
    w_bf = p.sb("w_bf", [128, 8 * C], BF16)
    q_bf = p.sb("q_bf", [128, 8 * T], BF16)
    k_bf = p.sb("k_bf", [128, 8 * T], BF16)
    v_sb = p.sb("v_sb", [128, 16 * 512], BF16)
    sg_bf = p.sb("sg_bf", [128, 4 * T], BF16)
    wh = p.sb("wh", [128, 8 * 512], BF16)
    woh = p.sb("woh", [128, 4 * C], BF16)
    cosb = p.sb("cosb", [128, T], F32)
    sinb = p.sb("sinb", [128, T], F32)
    gnws = p.sb("gnws", [128, 4], F32)
    masks = p.sb("masks", [128, 4 * 512], F32)
    ones1 = p.sb("ones1", [1, 128], F32)
    onesc = p.sb("onesc", [128, 1], F32)
    q1 = p.sb("q1", [128, 512], F32)
    q2 = p.sb("q2", [128, 512], F32)
    t1 = p.sb("t1", [128, 512], F32)
    t2 = p.sb("t2", [128, 512], F32)
    t3 = p.sb("t3", [128, 512], F32)
    lamb = p.sb("lamb", [128, 512], F32)
    lamb2 = p.sb("lamb2", [128, 512], F32)
    s_bf = p.sb("s_bf", [128, 512], BF16)
    s_bf2 = p.sb("s_bf2", [128, 512], BF16)
    s_bf3 = p.sb("s_bf3", [128, 512], BF16)
    srow = p.sb("srow", [1, 512], F32)
    rr = p.sb("rr", [1, 512], F32)
    rb = p.sb("rb", [128, 512], F32)

    ps_s = p.ps("ps_s", [128, 512], F32)
    ps_s2 = p.ps("ps_s2", [128, 512], F32)
    ps_o = p.ps("ps_o", [128, 2048], F32)
    ps_x = p.ps("ps_x", [128, 512], F32)
    ps_r = p.ps("ps_r", [1, 512], F32)

    p.dma("gpsimd", cosb[:, :], cosH, writes=["cosb"])
    p.dma("gpsimd", sinb[:, :], sinH, writes=["sinb"])
    p.dma("gpsimd", gnws[:, :], gnw, writes=["gnws"])
    for j in range(4):
        p.dma("gpsimd", masks[:, j * 512:(j + 1) * 512], mdel[j, :, :],
              writes=["masks"])
    p.memset(ones1[:, :], 1.0, writes=["ones1"])
    p.op("vector", lambda v: v.memset(onesc[:, :], 1.0), writes=["onesc"])
    for kc in range(8):
        p.dma("gpsimd", xn_bf[:, kc * T:(kc + 1) * T],
              xnF[kc * 128:(kc + 1) * 128, :], writes=["xn_bf"],
              chan="gpsimd#x" + str(kc % 2))

    for which, wT, dest, lam in (("q", wqT, q_bf, lamq_d),
                                 ("k", wkT, k_bf, lamk_d)):
        destkey = "q_bf" if which == "q" else "k_bf"
        for kc in range(8):
            p.dma("gpsimd", w_bf[:, kc * C:(kc + 1) * C],
                  wT[kc * 128:(kc + 1) * 128, :], writes=["w_bf"],
                  chan="gpsimd#x" + str(kc % 2))
        srows = [(srow, "srow"), (rr, "rr")]
        lambs = [(lamb, "lamb"), (lamb2, "lamb2")]
        qds = [(q1, "q1"), (q2, "q2"), (t3, "t3"), (rb, "rb")]
        for it, (h, tc) in enumerate([(h, tc) for h in range(4)
                                      for tc in range(4)]):
            tsl = slice(tc * 512, (tc + 1) * 512)
            sr, srk = srows[it % 2]
            lm, lmk = lambs[it % 2]
            p.dma("gpsimd", sr[:, :],
                  lam[0:1, h * T + tc * 512: h * T + (tc + 1) * 512],
                  writes=[srk], chan="gpsimd#x" + str(it % 2))
            qa, qak = qds[(it % 2) * 2]
            qb, qbk = qds[(it % 2) * 2 + 1]
            for half, dst, dkey in ((0, qa, qak), (1, qb, qbk)):
                ct = 2 * h + half
                psA, keyA = (ps_s, "ps_s") if half == 0 else (ps_s2, "ps_s2")
                for kc in range(8):
                    p.mm(psA[:, :],
                         w_bf[:, kc * C + ct * 128: kc * C + (ct + 1) * 128],
                         xn_bf[:, kc * T + tc * 512: kc * T + (tc + 1) * 512],
                         kc == 0, kc == 7, reads=["w_bf", "xn_bf"],
                         writes=[keyA])
                p.copy("scalar", dst[:, :], psA[:, :], reads=[keyA],
                       writes=[dkey])
            psB = ps_x[:, :] if it % 2 == 0 else ps_o[:, 0:512]
            psBk = "ps_x" if it % 2 == 0 else "ps_o"
            p.mm(psB, ones1[:, :], sr[:, :], True, True,
                 reads=["ones1", srk], writes=[psBk])
            p.copy("vector", lm[:, :], psB, reads=[psBk], writes=[lmk])
            p.tt(t1[:, :], qa[:, :], cosb[:, tsl], ALU.mult,
                 reads=[qak, "cosb"], writes=["t1"])
            p.tt(t2[:, :], qb[:, :], sinb[:, tsl], ALU.mult,
                 reads=[qbk, "sinb"], writes=["t2"])
            p.tt(t1[:, :], t1[:, :], t2[:, :], ALU.subtract,
                 reads=["t1", "t2"], writes=["t1"])
            d1sl = slice(2 * h * T + tc * 512, 2 * h * T + (tc + 1) * 512)
            p.tt(dest[:, d1sl], t1[:, :], lm[:, :], ALU.mult,
                 reads=["t1", lmk], writes=[destkey])
            p.tt(t1[:, :], qa[:, :], sinb[:, tsl], ALU.mult,
                 reads=[qak, "sinb"], writes=["t1"])
            p.tt(t2[:, :], qb[:, :], cosb[:, tsl], ALU.mult,
                 reads=[qbk, "cosb"], writes=["t2"])
            p.tt(t1[:, :], t1[:, :], t2[:, :], ALU.add,
                 reads=["t1", "t2"], writes=["t1"])
            d2sl = slice((2 * h + 1) * T + tc * 512,
                         (2 * h + 1) * T + (tc + 1) * 512)
            p.tt(dest[:, d2sl], t1[:, :], lm[:, :], ALU.mult,
                 reads=["t1", lmk], writes=[destkey])

    for h in range(4):
        for kc in range(8):
            p.dma("gpsimd", wh[:, kc * 512:(kc + 1) * 512],
                  wvT[kc * 128:(kc + 1) * 128, h * 512:(h + 1) * 512],
                  writes=["wh"], chan="gpsimd#x" + str(kc % 2))
        for tt in range(16):
            psA, keyA = (ps_s, "ps_s") if tt % 2 == 0 else (ps_s2, "ps_s2")
            for kc in range(8):
                p.mm(psA[:, :],
                     xn_bf[:, kc * T + tt * 128: kc * T + (tt + 1) * 128],
                     wh[:, kc * 512:(kc + 1) * 512], kc == 0, kc == 7,
                     reads=["xn_bf", "wh"], writes=[keyA])
            p.copy("scalar", v_sb[:, tt * 512:(tt + 1) * 512], psA[:, :],
                   reads=[keyA], writes=["v_sb"])
        for kc in range(8):
            p.dma("gpsimd", wh[:, kc * 512:(kc + 1) * 512],
                  wgT[kc * 128:(kc + 1) * 128, h * 512:(h + 1) * 512],
                  writes=["wh"], chan="gpsimd#x" + str(kc % 2))
        for gi, (gt, tc) in enumerate([(gt, tc) for gt in range(4)
                                       for tc in range(4)]):
            psA, keyA = (ps_s, "ps_s") if gi % 2 == 0 else (ps_s2, "ps_s2")
            for kc in range(8):
                p.mm(psA[:, :],
                     wh[:, kc * 512 + gt * 128: kc * 512 + (gt + 1) * 128],
                     xn_bf[:, kc * T + tc * 512: kc * T + (tc + 1) * 512],
                     kc == 0, kc == 7, reads=["wh", "xn_bf"],
                     writes=[keyA])
            p.act(sg_bf[:, gt * T + tc * 512: gt * T + (tc + 1) * 512],
                  psA[:, :], AF.Silu, reads=[keyA], writes=["sg_bf"])
        sbr = [(s_bf, "s_bf"), (s_bf2, "s_bf2"), (s_bf3, "s_bf3")]
        for qc in range(4):
            qsl0 = 2 * h * T + qc * 512
            qsl1 = (2 * h + 1) * T + qc * 512
            kt_last = qc * 4 + 3

            def sc_emit(kt, qsl0=qsl0, qsl1=qsl1, qc=qc):
                psA, keyA = (ps_s, "ps_s") if kt % 2 == 0 else (ps_s2, "ps_s2")
                p.mm(psA[:, :],
                     k_bf[:, 2 * h * T + kt * 128: 2 * h * T + (kt + 1) * 128],
                     q_bf[:, qsl0:qsl0 + 512], True, False,
                     reads=["k_bf", "q_bf"], writes=[keyA])
                p.mm(psA[:, :],
                     k_bf[:, (2 * h + 1) * T + kt * 128:
                          (2 * h + 1) * T + (kt + 1) * 128],
                     q_bf[:, qsl1:qsl1 + 512], False, True,
                     reads=["k_bf", "q_bf"], writes=[keyA])
                delta = qc * 512 - kt * 128
                sb_t, sb_k = sbr[kt % 3]
                if delta >= 128:
                    p.copy("scalar", sb_t[:, :], psA[:, :], reads=[keyA],
                           writes=[sb_k])
                else:
                    j = (delta + 384) // 128
                    p.tt(sb_t[:, :], psA[:, :], masks[:, j * 512:(j + 1) * 512],
                         ALU.mult, reads=[keyA, "masks"], writes=[sb_k])

            def sv_emit(kt, kt_last=kt_last):
                sb_t, sb_k = sbr[kt % 3]
                for dvt in range(4):
                    p.mm(ps_o[:, dvt * 512:(dvt + 1) * 512],
                         v_sb[:, kt * 512 + dvt * 128: kt * 512 + (dvt + 1) * 128],
                         sb_t[:, :], kt == 0, kt == kt_last,
                         reads=["v_sb", sb_k], writes=["ps_o"])

            pend = []
            for kt in range(kt_last + 1):
                sc_emit(kt)
                pend.append(kt)
                if kt >= 1:
                    sv_emit(pend.pop(0))
            for kt in pend:
                sv_emit(kt)
            for dvt in range(4):
                p.act(t2[:, :], ps_o[:, dvt * 512:(dvt + 1) * 512], AF.Square,
                      reads=["ps_o"], writes=["t2"])
                p.mm(ps_r[:, :], onesc[:, :], t2[:, :], dvt == 0, dvt == 3,
                     reads=["onesc", "t2"], writes=["ps_r"])
            p.copy("vector", srow[:, :], ps_r[:, :], reads=["ps_r"],
                   writes=["srow"])
            p.act(srow[:, :], srow[:, :], AF.Sqrt, bias=1e-5, scale=1.0 / 512,
                  reads=["srow"], writes=["srow"])
            p.recip(rr[:, :], srow[:, :], reads=["srow"], writes=["rr"])
            p.mm(ps_x[:, :], ones1[:, :], rr[:, :], True, True,
                 reads=["ones1", "rr"], writes=["ps_x"])
            p.copy("vector", rb[:, :], ps_x[:, :], reads=["ps_x"],
                   writes=["rb"])
            for dvt in range(4):
                p.stt(s_bf[:, :], ps_o[:, dvt * 512:(dvt + 1) * 512],
                      gnws[:, dvt:dvt + 1], rb[:, :], ALU.mult, ALU.mult,
                      reads=["ps_o", "gnws", "rb"], writes=["s_bf"])
                osl = slice(dvt * T + qc * 512, dvt * T + (qc + 1) * 512)
                p.tt(w_bf[:, osl], s_bf[:, :], sg_bf[:, osl], ALU.mult,
                     reads=["s_bf", "sg_bf"], writes=["w_bf"])
        for fs in range(4):
            p.dma("gpsimd", woh[:, fs * C:(fs + 1) * C],
                  woT[h * 512 + fs * 128: h * 512 + (fs + 1) * 128, :],
                  writes=["woh"], chan="gpsimd#x" + str(fs % 2))
        for oi, (ct, tc) in enumerate([(ct, tc) for ct in range(8)
                                       for tc in range(4)]):
            psA, keyA = (ps_s, "ps_s") if oi % 2 == 0 else (ps_s2, "ps_s2")
            stg, sgk = (t1, "t1") if oi % 2 == 0 else (t3, "t3")
            for fs in range(4):
                p.mm(psA[:, :],
                     woh[:, fs * C + ct * 128: fs * C + (ct + 1) * 128],
                     w_bf[:, fs * T + tc * 512: fs * T + (tc + 1) * 512],
                     fs == 0, fs == 3, reads=["woh", "w_bf"],
                     writes=[keyA])
            p.copy("scalar", stg[:, :], psA[:, :], reads=[keyA],
                   writes=[sgk])
            dsl = (slice(ct * 128, (ct + 1) * 128),
                   slice(tc * 512, (tc + 1) * 512))
            okey = "s1o%d" % oi
            if h == 0:
                p.dma("gpsimd", s1F_o[dsl[0], dsl[1]], stg[:, :],
                      reads=[sgk], writes=[okey],
                      chan="gpsimd#o" + str(oi % 2))
            else:
                p.op("gpsimd",
                     lambda g, d0=dsl[0], d1=dsl[1], s=stg: g.dma_start(
                         s1F_o[d0, d1], s[:, :], accum_op=ALU.add),
                     reads=[sgk, okey], writes=[okey], dma=True,
                     chan="gpsimd#o" + str(oi % 2))



_cache = {}


def _ct(a):
    return np.ascontiguousarray(a, dtype=np.float32)


def _rope_tables_interleaved():
    d = np.arange(128) % 64
    inv = (1.0 / 10000.0 ** ((d // 2 * 2) / 64.0))[:, None]
    pos = np.arange(T)[None, :]
    fr = pos * inv
    return _ct(np.cos(fr)), _ct(np.sin(fr))


def _rope_tables_half():
    inv = (1.0 / 10000.0 ** (np.arange(0, 256, 2) / 256.0))[:, None]
    pos = np.arange(T)[None, :]
    fr = pos * inv
    return _ct(np.cos(fr)), _ct(np.sin(fr))


def _p128():
    P = np.zeros((128, 128), np.float32)
    for blk in range(2):
        for i in range(64):
            gi = blk * 64 + i
            if i % 2 == 0:
                P[blk * 64 + i + 1, gi] = -1.0
            else:
                P[blk * 64 + i - 1, gi] = 1.0
    return P


def build_fused():
    nc = bass.Bass("TRN2", target_bir_lowering=False, debug=False)
    EI = "ExternalInput"

    def di(name, shape):
        return nc.dram_tensor(name, list(shape), F32, kind=EI).ap()

    def dint(name, shape):
        return nc.dram_tensor(name, list(shape), F32, kind="Internal").ap()

    x = di("x", [T, C])
    xF_in = di("xF_in", [C, T])
    eyef = di("eyef", [128, 128])
    ln_g = di("ln_g", [1, C])
    ln_b = di("ln_b", [1, C])
    wq0T = di("wq0T", [C, C])
    wk0T = di("wk0T", [C, C])
    wv0T = di("wv0T", [C, C])
    wo0T = di("wo0T", [C, C])
    s_col = di("s_col", [128, 8])
    cosI = di("cosI", [128, T])
    sinI = di("sinI", [128, T])
    P128 = di("P128", [128, 128])
    onesH = di("onesH", [128, 2])
    onesHT = di("onesHT", [2, 128])
    Wr = di("Wr", [C, 8])
    br = di("br", [1, 8])
    W1 = di("W1", [8, C, 4096])
    B1 = di("B1", [8, 128, 32])
    W2 = di("W2", [8, 4096, C])
    B2 = di("B2", [8, C])
    norm_w = di("norm_w", [1, C])
    zeros_c = di("zeros_c", [1, C])
    r1WT = di("r1WT", [C, 8])
    r1b = di("r1b", [1, 8])
    wq1T = di("wq1T", [C, C])
    wk1T = di("wk1T", [C, C])
    wv1T = di("wv1T", [C, 2048])
    wg1T = di("wg1T", [C, 2048])
    wo1T = di("wo1T", [2048, C])
    gnw = di("gnw", [128, 4])
    cosH = di("cosH", [128, T])
    sinH = di("sinH", [128, T])
    lamq = di("lamq", [1, 4 * T])
    lamk = di("lamk", [1, 4 * T])
    mdel = di("mdel", [4, 128, 512])
    F1 = di("F1", [8, C, 4096])
    Fb1 = di("Fb1", [8, 128, 32])
    F2 = di("F2", [8, 4096, C])
    Fb2 = di("Fb2", [8, C])
    blk16 = di("blk16", [128, 8])
    blk16T = di("blk16T", [8, 128])

    xnF = dint("xnF", [C, T])
    xinF = dint("xinF", [C, T])
    mgT0 = dint("mgT0", [8, T])
    s0F = dint("s0F", [C, T])
    g0F = dint("g0F", [C, T])
    x1TM = dint("x1TM", [T, C])
    xn1F = dint("xn1F", [C, T])
    x1F = dint("x1F", [C, T])
    xin1F = dint("xin1F", [C, T])
    s1F = dint("s1F", [C, T])
    g1F = dint("g1F", [C, T])
    mg_scr = dint("mg_scr", [8, T])
    probsT_h = nc.dram_tensor("probsTb", [8, T], F32)
    pAll_h = nc.dram_tensor("pAllb", [64, T], F32)
    probsT1 = probsT_h.ap()
    pAll = pAll_h.ap()

    out = nc.dram_tensor("out", [T, C], F32, kind="ExternalOutput").ap()

    p = Prog("n0", nc=nc)
    emit_norm(p, "ln2", x, ln_g, ln_b, eyef, xnF, xinF)
    p.finish()
    nc.reset()

    p = Prog("r0", nc=nc)
    emit_router(p, xnF, Wr, br, eyef, mgT_o=mgT0)
    p.finish()
    nc.reset()

    p = Prog("a0", nc=nc)
    emit_attn0(p, xnF, wq0T, wk0T, wv0T, wo0T, s_col, cosI, sinI, P128,
               onesH, onesHT, eyef, s0F)
    p.finish()
    nc.reset()

    p = Prog("m0", nc=nc)
    emit_moe(p, False, xinF, W1, B1, W2, B2, g0F, mgT_src=mgT0)
    p.finish()
    nc.reset()

    p = Prog("c0", nc=nc)
    emit_combine(p, xF_in, s0F, g0F, eyef, oF=x1F, oTM=x1TM)
    p.finish()
    nc.reset()

    p = Prog("n1", nc=nc)
    emit_norm(p, "rms", x1TM, norm_w, zeros_c, eyef, xn1F, xin1F)
    p.finish()
    nc.reset()

    p = Prog("r1", nc=nc)
    emit_router(p, xin1F, r1WT, r1b, eyef, probsT_o=probsT1)
    p.finish()
    nc.reset()

    p = Prog("cc", nc=nc)
    p.op("gpsimd", lambda g: g.collective_compute(
        "AllGather", ALU.bypass, replica_groups=[list(range(NCORE))],
        ins=[probsT_h.ap().opt()], outs=[pAll_h.ap().opt()]),
        writes=["pAll"])
    p.finish()
    nc.reset()

    p = Prog("a1", nc=nc)
    emit_attn1(p, xn1F, wq1T, wk1T, wv1T, wg1T, wo1T, gnw, cosH, sinH,
               lamq, lamk, mdel, s1F)
    p.finish()
    nc.reset()

    p = Prog("m1", nc=nc)
    emit_moe(p, True, xin1F, F1, Fb1, F2, Fb2, g1F, agout=pAll,
             probsT_mine=probsT1, blk16=blk16, blk16T=blk16T,
             mg_scratch=mg_scr)
    p.finish()
    nc.reset()

    p = Prog("c1", nc=nc)
    emit_combine(p, x1F, s1F, g1F, eyef, oTM=out)
    p.finish()
    return nc


def kernel(**inp):
    x_full = np.asarray(inp["L0_attn_x"], np.float32)

    wq0T = _ct(inp["L0_attn_Wq"].T)
    wk0T = _ct(inp["L0_attn_Wk"].T)
    wv0T = _ct(inp["L0_attn_Wv"].T)
    wo0T = _ct(inp["L0_attn_Wo"].T)
    s_col = _ct(np.asarray(inp["L0_attn_s_qk"], np.float32).reshape(8, 128).T)
    cosI, sinI = _rope_tables_interleaved()
    P128v = _p128()
    onesH = np.zeros((128, 2), np.float32)
    onesH[:64, 0] = 1.0
    onesH[64:, 1] = 1.0
    onesHT = _ct(onesH.T)
    eyef = np.eye(128, dtype=np.float32)

    Wr = _ct(inp["L0_ffn_Wr"])
    br = _ct(np.asarray(inp["L0_ffn_br"]).reshape(1, 8))
    ln_g = _ct(np.asarray(inp["L0_ffn_ln_g"]).reshape(1, C))
    ln_b = _ct(np.asarray(inp["L0_ffn_ln_b"]).reshape(1, C))
    W1 = _ct(inp["L0_ffn_W1"])
    B1 = _ct(np.asarray(inp["L0_ffn_b1"]).reshape(8, 32, 128).transpose(0, 2, 1))
    W2 = _ct(inp["L0_ffn_W2"])
    B2 = _ct(inp["L0_ffn_b2"])

    wq1T = _ct(inp["L1_attn_Wq"].T)
    wk1T = _ct(inp["L1_attn_Wk"].T)
    wv1T = _ct(inp["L1_attn_Wv"].T)
    wg1T = _ct(inp["L1_attn_Wg"].T)
    wo1T = _ct(inp["L1_attn_Wo"].T)
    gnw = _ct(np.asarray(inp["L1_attn_g_norm_w"]).reshape(4, 128).T)
    cosH, sinH = _rope_tables_half()
    log_decay = np.log1p(-np.exp2(-5.0 - np.arange(4, dtype=np.float64)))
    tpos = np.arange(T, dtype=np.float64)
    lamq = _ct((np.exp(log_decay[:, None] * tpos[None, :])
                * (256.0 ** -0.5)).reshape(1, -1))
    lamk = _ct(np.exp(-log_decay[:, None] * tpos[None, :]).reshape(1, -1))
    mdel = np.zeros((4, 128, 512), np.float32)
    for j, delta in enumerate((-384, -256, -128, 0)):
        jj = np.arange(512)[None, :] + delta
        pp = np.arange(128)[:, None]
        mdel[j] = (jj >= pp).astype(np.float32)
    mdel = _ct(mdel)

    norm_w = _ct(np.asarray(inp["L1_ffn_norm_weight"]).reshape(1, C))
    zeros_c = np.zeros((1, C), np.float32)
    r1WT = _ct(inp["L1_ffn_router_w"].T)
    r1b = _ct(np.asarray(inp["L1_ffn_router_b"]).reshape(1, 8))
    F1 = _ct(np.transpose(inp["L1_ffn_fc1_w"], (0, 2, 1)))
    Fb1 = _ct(np.asarray(inp["L1_ffn_fc1_b"]).reshape(8, 32, 128).transpose(0, 2, 1))
    F2 = _ct(np.transpose(inp["L1_ffn_fc2_w"], (0, 2, 1)))
    Fb2 = _ct(inp["L1_ffn_fc2_b"])
    blk16 = np.zeros((128, 8), np.float32)
    for r in range(128):
        blk16[r, (r // 2) % 8] = 1.0
    blk16T = _ct(blk16.T)

    if "nc" not in _cache:
        _cache["nc"] = build_fused()
    nc = _cache["nc"]

    shared = dict(eyef=eyef, ln_g=ln_g, ln_b=ln_b, wq0T=wq0T, wk0T=wk0T,
                  wv0T=wv0T, wo0T=wo0T, s_col=s_col, cosI=cosI, sinI=sinI,
                  P128=P128v, onesH=onesH, onesHT=onesHT, Wr=Wr, br=br,
                  W1=W1, B1=B1, W2=W2, B2=B2, norm_w=norm_w, zeros_c=zeros_c,
                  r1WT=r1WT, r1b=r1b, wq1T=wq1T, wk1T=wk1T, wv1T=wv1T,
                  wg1T=wg1T, wo1T=wo1T, gnw=gnw, cosH=cosH, sinH=sinH,
                  lamq=lamq, lamk=lamk, mdel=mdel,
                  F1=F1, Fb1=Fb1, F2=F2, Fb2=Fb2, blk16=blk16, blk16T=blk16T)
    in_maps = []
    for c in range(NCORE):
        m = dict(shared)
        m["x"] = _ct(x_full[c])
        m["xF_in"] = _ct(x_full[c].T)
        in_maps.append(m)
    res = run_bass_kernel_spmd(nc, in_maps, core_ids=list(range(NCORE)))
    out = np.stack([np.asarray(r["out"]) for r in res.results], axis=0)
    return out.astype(np.float32)
